# version 12
# baseline (speedup 1.0000x reference)
import numpy as np
import ml_dtypes
from contextlib import ExitStack

import concourse.bass as bass
import concourse.mybir as mybir
from concourse.bass_utils import run_bass_kernel_spmd

F32 = mybir.dt.float32
BF16 = mybir.dt.bfloat16
I32 = mybir.dt.int32
AF = mybir.ActivationFunctionType
ALU = mybir.AluOpType

NC = 8
D = 2048
TOK = 1024
DC = 16
TT8 = 8
FULLT = 4096
INTER = 5504
OT43 = 43
GSH = 768
DSH = 256
HQ, HKV, DH = 32, 8, 64
EPS = 1e-5

STAGES = 4


class Sem:
    def __init__(self, nc, ctx, name):
        self.h = ctx.enter_context(nc.semaphore(name))
        self.n = 0

    def inc(self, instr, k=1):
        instr.then_inc(self.h, k)
        self.n += k
        return self.n


def build_nc(stages=None):
    if stages is None:
        stages = STAGES
    nc = bass.Bass("TRN2", target_bir_lowering=False)

    xs = nc.dram_tensor("xs", [D, TOK], F32, kind="ExternalInput").ap()
    wq0 = nc.dram_tensor("wq0", [D, D], F32, kind="ExternalInput").ap()
    wk0 = nc.dram_tensor("wk0", [D, 512], F32, kind="ExternalInput").ap()
    wv0 = nc.dram_tensor("wv0", [D, 512], F32, kind="ExternalInput").ap()
    wo0 = nc.dram_tensor("wo0", [D, D], F32, kind="ExternalInput").ap()
    lut = nc.dram_tensor("lut", [16], F32, kind="ExternalInput").ap()
    gidx = nc.dram_tensor("gidx", [D, GSH], I32, kind="ExternalInput").ap()
    uidx = nc.dram_tensor("uidx", [D, GSH], I32, kind="ExternalInput").ap()
    didx = nc.dram_tensor("didx", [INTER, DSH], I32, kind="ExternalInput").ap()
    wq1 = nc.dram_tensor("wq1", [D, 64], F32, kind="ExternalInput").ap()
    wk1 = nc.dram_tensor("wk1", [D, 64], F32, kind="ExternalInput").ap()
    wv1 = nc.dram_tensor("wv1", [D, D], F32, kind="ExternalInput").ap()
    wo1 = nc.dram_tensor("wo1", [D, D], F32, kind="ExternalInput").ap()
    wup = nc.dram_tensor("wup", [D, D], F32, kind="ExternalInput").ap()
    wdn = nc.dram_tensor("wdn", [D, D], F32, kind="ExternalInput").ap()
    wg8 = nc.dram_tensor("wg8", [D, 8], F32, kind="ExternalInput").ap()
    wex = nc.dram_tensor("wex", [D, D], F32, kind="ExternalInput").ap()
    ys = nc.dram_tensor("ys", [D, TOK], F32, kind="ExternalOutput").ap()

    wgt_in = nc.dram_tensor("wgt_in", [D, GSH], BF16)
    wut_in = nc.dram_tensor("wut_in", [D, GSH], BF16)
    wdt_in = nc.dram_tensor("wdt_in", [INTER, DSH], BF16)
    wgt_all = nc.dram_tensor("wgt_all", [NC * D, GSH], BF16)
    wut_all = nc.dram_tensor("wut_all", [NC * D, GSH], BF16)
    wdt_all = nc.dram_tensor("wdt_all", [NC * INTER, DSH], BF16)
    k0_in = nc.dram_tensor("k0_in", [512, TOK], BF16)
    v0_in = nc.dram_tensor("v0_in", [TOK, 512], BF16)
    k0_all = nc.dram_tensor("k0_all", [4 * 512, TOK], BF16)
    v0_all = nc.dram_tensor("v0_all", [4 * TOK, 512], BF16)
    kf_in = nc.dram_tensor("kf_in", [64, TOK], BF16)
    v1_in = nc.dram_tensor("v1_in", [TOK, D], BF16)
    kf_all = nc.dram_tensor("kf_all", [4 * 64, TOK], BF16)
    v1_all = nc.dram_tensor("v1_all", [4 * TOK, D], BF16)

    ctx = ExitStack()
    E = ctx.enter_context
    _sc = [None]
    def sc_open(name):
        if _sc[0]: nc.leave_named_scope(_sc[0][0], _sc[0][1], False)
        sid, _ = nc.enter_named_scope(name, False)
        _sc[0] = (name, sid)
    def sc_close():
        if _sc[0]:
            nc.leave_named_scope(_sc[0][0], _sc[0][1], False); _sc[0] = None

    x_fm = E(nc.sbuf_tensor("x_fm", [128, DC * TOK], F32))
    h_bf = E(nc.sbuf_tensor("h_bf", [128, DC * TOK], BF16))
    ones_bf = E(nc.sbuf_tensor("ones_bf", [128, 128], BF16))
    ident_bf = E(nc.sbuf_tensor("ident_bf", [128, 128], BF16))
    lut128 = E(nc.sbuf_tensor("lut128", [128, 16], F32))
    lut_bf = E(nc.sbuf_tensor("lut_bf", [1, 16], BF16))
    lut_f = E(nc.sbuf_tensor("lut_f", [1, 16], F32))
    it_i = E(nc.sbuf_tensor("it_i", [128, 128], I32))
    ip_i = E(nc.sbuf_tensor("ip_i", [128, 1], I32))
    it_f = E(nc.sbuf_tensor("it_f", [128, 128], F32))
    ip_f = E(nc.sbuf_tensor("ip_f", [128, 1], F32))
    mean_sb = E(nc.sbuf_tensor("mean_sb", [128, TOK], F32))
    rstd_sb = E(nc.sbuf_tensor("rstd_sb", [128, TOK], F32))
    tmp_sb = E(nc.sbuf_tensor("tmp_sb", [128, TOK], F32))
    rcp_sb = E(nc.sbuf_tensor("rcp_sb", [128, TOK], F32))

    ld = Sem(nc, ctx, "ld")
    wld = Sem(nc, ctx, "wld")
    gld = Sem(nc, ctx, "gld")
    pe = Sem(nc, ctx, "pe")
    ac = Sem(nc, ctx, "ac")
    dv = Sem(nc, ctx, "dv")
    gp = Sem(nc, ctx, "gp")
    cc = Sem(nc, ctx, "cc")
    st = Sem(nc, ctx, "st")

    sc_open("p0")
    with nc.Block() as blk:
        def _sync(sync):
            for c in range(DC):
                ld.inc(sync.dma_start(out=x_fm[:, c * TOK:(c + 1) * TOK],
                                      in_=xs[c * 128:(c + 1) * 128, :]), 16)
            ld.inc(sync.dma_start(out=lut_f[:], in_=lut[None, :]), 16)
        blk.sync(_sync)

        def _gp(g):
            gp.inc(g.iota(it_i[:], pattern=[[1, 128]], base=0, channel_multiplier=0))
            gp.inc(g.iota(ip_i[:], pattern=[[0, 1]], base=0, channel_multiplier=1))
            g.wait_ge(gp.h, gp.n)
            gp.inc(g.tensor_copy(out=it_f[:], in_=it_i[:]))
            gp.inc(g.tensor_copy(out=ip_f[:], in_=ip_i[:]))
        blk.gpsimd(_gp)

        def _v(v):
            v.memset(ones_bf[:], 1.0)
            v.wait_ge(gp.h, gp.n)
            v.tensor_scalar(out=ident_bf[:], in0=it_f[:], scalar1=ip_f[:],
                            scalar2=None, op0=ALU.is_equal)
            v.wait_ge(ld.h, ld.n)
            dv.inc(v.tensor_copy(out=lut_bf[:], in_=lut_f[:]))
        blk.vector(_v)
    with (nc.psum_tensor("pslut", [128, 16], F32) as pslut, nc.Block() as blk):
        def _t(t):
            pe.inc(t.matmul(pslut[:], ones_bf[0:1, :], lut_bf[:], start=True, stop=True))
        blk.tensor(_t)

        def _s(s):
            s.wait_ge(pe.h, pe.n)
            ac.inc(s.copy(out=lut128[:], in_=pslut[:]))
        blk.scalar(_s)

    def emit_ln(psA, psB, xb, xsq):
        with nc.Block() as blk:
            def _v(v):
                dv.inc(v.tensor_copy(out=xb, in_=x_fm[:]))
            blk.vector(_v)

            def _s(s):
                ac.inc(s.activation(out=xsq, in_=x_fm[:], func=AF.Square))
            blk.scalar(_s)

            dvn, acn = dv.n, ac.n
            def _t(t):
                t.wait_ge(dv.h, dvn)
                t.wait_ge(ac.h, acn)
                for src, ps in ((xb, psA), (xsq, psB)):
                    for qc in range(2):
                        for c in range(DC):
                            i = t.matmul(ps[:, qc * 512:(qc + 1) * 512], ones_bf[:],
                                         src[:, c * TOK + qc * 512: c * TOK + (qc + 1) * 512],
                                         start=(c == 0), stop=(c == DC - 1))
                        pe.inc(i)
            blk.tensor(_t)
        with nc.Block() as blk:
            def _v(v):
                v.tensor_scalar(out=mean_sb[:], in0=psA[:, 0:TOK], scalar1=1.0 / D,
                                scalar2=None, op0=ALU.mult)
                v.tensor_scalar(out=rstd_sb[:], in0=psB[:, 0:TOK], scalar1=1.0 / D,
                                scalar2=None, op0=ALU.mult)
                v.tensor_tensor(out=tmp_sb[:], in0=mean_sb[:], in1=mean_sb[:], op=ALU.mult)
                v.tensor_tensor(out=rcp_sb[:], in0=rstd_sb[:], in1=tmp_sb[:], op=ALU.subtract)
                dv.inc(v.tensor_scalar_add(out=rstd_sb[:], in0=rcp_sb[:], scalar1=EPS))
                v.wait_ge(ac.h, ac.n + 1)
                for c in range(DC):
                    v.tensor_tensor(out=tmp_sb[:], in0=x_fm[:, c * TOK:(c + 1) * TOK],
                                    in1=mean_sb[:], op=ALU.subtract)
                    i = v.tensor_tensor(out=h_bf[:, c * TOK:(c + 1) * TOK],
                                        in0=tmp_sb[:], in1=rstd_sb[:], op=ALU.mult)
                dv.inc(i)
            blk.vector(_v)

            dvn = dv.n - 1
            def _s(s):
                s.wait_ge(dv.h, dvn)
                s.activation(out=psA[:, 0:TOK], in_=rstd_sb[:], func=AF.Ln)
                ac.inc(s.activation(out=rstd_sb[:], in_=psA[:, 0:TOK], func=AF.Exp, scale=-0.5))
            blk.scalar(_s)

    def emit_proj(w_ap, OW, wf32, wbuf, psP, evac, wait_sems=(), rhs=None):
        TW = 128
        not_ = OW // TW
        rh = rhs if rhs is not None else h_bf
        with nc.Block() as blk:
            wld0, pe0, gp0, dv0 = wld.n, pe.n, gp.n, dv.n

            def _sync(sync):
                for sh, val in wait_sems:
                    sync.wait_ge(sh, val)
                for ot in range(not_):
                    if ot >= 2:
                        sync.wait_ge(pe.h, pe0 + 2 * (ot - 1))
                    wld.inc(sync.dma_start(
                        out=wf32[:, (ot % 2) * (DC * TW):(ot % 2 + 1) * (DC * TW)],
                        in_=bass.AP(w_ap.tensor, w_ap.offset + ot * TW,
                                    [[OW, 128], [128 * OW, DC], [1, TW]])), 16)
            blk.sync(_sync)

            def _gp(g):
                for ot in range(not_):
                    g.wait_ge(wld.h, wld0 + 16 * (ot + 1))
                    if ot >= 2:
                        g.wait_ge(pe.h, pe0 + 2 * (ot - 1))
                    gp.inc(g.tensor_copy(
                        out=wbuf[:, (ot % 2) * (DC * TW):(ot % 2 + 1) * (DC * TW)],
                        in_=wf32[:, (ot % 2) * (DC * TW):(ot % 2 + 1) * (DC * TW)]))
            blk.gpsimd(_gp)

            def _t(t):
                for ot in range(not_):
                    t.wait_ge(gp.h, gp0 + ot + 1)
                    pb = ot % 2
                    if ot >= 2:
                        t.wait_ge(dv.h, dv0 + 2 * (ot - 2) + 2)
                    for qc in range(2):
                        for c in range(DC):
                            i = t.matmul(
                                psP[:, pb * 1024 + qc * 512: pb * 1024 + (qc + 1) * 512],
                                wbuf[:, (ot % 2) * (DC * TW) + c * TW:
                                     (ot % 2) * (DC * TW) + (c + 1) * TW],
                                rh[:, c * TOK + qc * 512: c * TOK + (qc + 1) * 512],
                                start=(c == 0), stop=(c == DC - 1))
                        pe.inc(i)
            blk.tensor(_t)

            def _v(v):
                for ot in range(not_):
                    pb = ot % 2
                    for qc in range(2):
                        v.wait_ge(pe.h, pe0 + 2 * ot + qc + 1)
                        dv.inc(evac(v, ot, qc,
                                    psP[:, pb * 1024 + qc * 512: pb * 1024 + (qc + 1) * 512]))
            blk.vector(_v)

    with ExitStack() as s1:
        E1 = s1.enter_context
        q_fm = E1(nc.sbuf_tensor("q_fm", [128, DC * TOK], BF16))
        s1b = s1.enter_context(ExitStack())
        E1b = s1b.enter_context
        wf32 = E1b(nc.sbuf_tensor("wf32", [128, 2 * DC * 128], F32))
        wbuf = E1b(nc.sbuf_tensor("wbuf", [128, 2 * DC * 128], BF16))
        ksh = E1b(nc.sbuf_tensor("ksh", [128, 4 * TOK], BF16))
        vsh = E1b(nc.sbuf_tensor("vsh", [128, 8 * 512], BF16))
        midx = E1b(nc.sbuf_tensor("midx", [128, GSH], I32))
        mbf = E1b(nc.sbuf_tensor("mbf", [128, GSH], BF16))
        mterm = E1b(nc.sbuf_tensor("mterm", [128, GSH], BF16))
        macc = E1b(nc.sbuf_tensor("macc", [128, 2 * GSH], BF16))

        sc_open("ln1")
        with (nc.psum_tensor("psA", [128, 1024], F32) as psA,
              nc.psum_tensor("psB", [128, 1024], F32) as psB):
            emit_ln(psA, psB, q_fm[:, 0:DC * TOK], h_bf[:, 0:DC * TOK])

        sc_open("qkv0")
        with nc.psum_tensor("psP", [128, 2048], F32) as psP:
            def evq(v, ot, qc, pss):
                return v.tensor_copy(out=q_fm[:, ot * TOK + qc * 512: ot * TOK + (qc + 1) * 512],
                                     in_=pss)
            emit_proj(wq0, D, wf32, wbuf, psP, evq)

            def evk(v, ot, qc, pss):
                return v.tensor_copy(out=ksh[:, ot * TOK + qc * 512: ot * TOK + (qc + 1) * 512],
                                     in_=pss)
            emit_proj(wk0, 512, wf32, wbuf, psP, evk)

        with nc.Block() as blk:
            wld0, pe0, gp0 = wld.n, pe.n, gp.n

            def _sync(sync):
                for q4 in range(4):
                    if q4 >= 2:
                        sync.wait_ge(pe.h, pe0 + q4 - 1)
                    wld.inc(sync.dma_start(
                        out=wf32[:, (q4 % 2) * 2048:(q4 % 2 + 1) * 2048],
                        in_=bass.AP(wv0.tensor, wv0.offset + q4 * 4 * 128 * 512,
                                    [[512, 128], [128 * 512, 4], [1, 512]])), 16)
            blk.sync(_sync)

            def _gp(g):
                for q4 in range(4):
                    g.wait_ge(wld.h, wld0 + 16 * (q4 + 1))
                    if q4 >= 2:
                        g.wait_ge(pe.h, pe0 + q4 - 1)
                    gp.inc(g.tensor_copy(out=wbuf[:, (q4 % 2) * 2048:(q4 % 2 + 1) * 2048],
                                         in_=wf32[:, (q4 % 2) * 2048:(q4 % 2 + 1) * 2048]))
            blk.gpsimd(_gp)

            with nc.psum_tensor("psV", [128, 8 * 512], F32) as psV:
                def _t(t):
                    for q4 in range(4):
                        t.wait_ge(gp.h, gp0 + q4 + 1)
                        for tt in range(TT8):
                            for c4 in range(4):
                                c = q4 * 4 + c4
                                i = t.matmul(psV[:, tt * 512:(tt + 1) * 512],
                                             h_bf[:, c * TOK + tt * 128: c * TOK + (tt + 1) * 128],
                                             wbuf[:, (q4 % 2) * 2048 + c4 * 512:(q4 % 2) * 2048 + (c4 + 1) * 512],
                                             start=(c == 0), stop=(c == DC - 1))
                        pe.inc(i)
                blk.tensor(_t)

                def _v(v):
                    v.wait_ge(pe.h, pe0 + 4)
                    for tt in range(TT8):
                        dv.inc(v.tensor_copy(out=vsh[:, tt * 512:(tt + 1) * 512],
                                             in_=psV[:, tt * 512:(tt + 1) * 512]))
                blk.vector(_v)

        sc_open("kvship_mat")
        with nc.Block() as blk:
            base_dv = dv.n

            def _gp(g):
                for p in range(4):
                    gld.inc(g.dma_start(out=k0_in[p * 128:(p + 1) * 128, :],
                                        in_=ksh[:, p * TOK:(p + 1) * TOK]), 16)
                for tt in range(TT8):
                    gld.inc(g.dma_start(out=v0_in[tt * 128:(tt + 1) * 128, :],
                                        in_=vsh[:, tt * 512:(tt + 1) * 512]), 16)
                g.wait_ge(gld.h, gld.n)
                cc.inc(g.collective_compute(
                    "AllGather", ALU.bypass,
                    replica_groups=[[0, 1, 2, 3], [4, 5, 6, 7]],
                    ins=[k0_in.ap().opt()], outs=[k0_all.ap().opt()]))
                cc.inc(g.collective_compute(
                    "AllGather", ALU.bypass,
                    replica_groups=[[0, 1, 2, 3], [4, 5, 6, 7]],
                    ins=[v0_in.ap().opt()], outs=[v0_all.ap().opt()]))
            blk.gpsimd(_gp)
            cc_kv = cc.n

            chunks = ([("g", c) for c in range(DC)] + [("u", c) for c in range(DC)]
                      + [("d", c) for c in range(OT43)])
            base_gld = gld.n
            base_ld = ld.n

            def _sync(sync):
                for i, (kind, c) in enumerate(chunks):
                    w = GSH if kind in "gu" else DSH
                    src = {"g": gidx, "u": uidx, "d": didx}[kind]
                    if i >= 1:
                        sync.wait_ge(dv.h, base_dv + 2 * i - 1)
                    ld.inc(sync.dma_start(out=midx[:, 0:w],
                                          in_=src[c * 128:(c + 1) * 128, :]), 16)
            blk.sync(_sync)

            def _v(v):
                for i, (kind, c) in enumerate(chunks):
                    w = GSH if kind in "gu" else DSH
                    sl = (i % 2) * GSH
                    v.wait_ge(ld.h, base_ld + 16 * (i + 1))
                    if i >= 2:
                        v.wait_ge(gld.h, base_gld + 16 * (i - 1))
                    dv.inc(v.tensor_copy(out=mbf[:, 0:w], in_=midx[:, 0:w]))
                    for j in range(16):
                        v.tensor_scalar(out=mterm[:, 0:w], in0=mbf[:, 0:w],
                                        scalar1=float(j), scalar2=lut128[:, j:j + 1],
                                        op0=ALU.is_equal, op1=ALU.mult)
                        if j == 0:
                            v.tensor_copy(out=macc[:, sl:sl + w], in_=mterm[:, 0:w])
                        elif j == 15:
                            dv.inc(v.tensor_tensor(out=macc[:, sl:sl + w],
                                                   in0=macc[:, sl:sl + w],
                                                   in1=mterm[:, 0:w], op=ALU.add))
                        else:
                            v.tensor_tensor(out=macc[:, sl:sl + w], in0=macc[:, sl:sl + w],
                                            in1=mterm[:, 0:w], op=ALU.add)
            blk.vector(_v)

            def _s(s):
                for i, (kind, c) in enumerate(chunks):
                    w = GSH if kind in "gu" else DSH
                    sl = (i % 2) * GSH
                    dst = {"g": wgt_in, "u": wut_in, "d": wdt_in}[kind]
                    s.wait_ge(dv.h, base_dv + 2 * (i + 1))
                    gld.inc(s.dma_start(out=dst[c * 128:(c + 1) * 128, :],
                                        in_=macc[:, sl:sl + w]), 16)
            blk.scalar(_s)

        sc_open("wag")
        with nc.Block() as blk:
            def _gp(g):
                g.wait_ge(gld.h, gld.n)
                cc.inc(g.collective_compute(
                    "AllGather", ALU.bypass, replica_groups=[list(range(NC))],
                    ins=[wgt_in.ap().opt()], outs=[wgt_all.ap().opt()]))
                cc.inc(g.collective_compute(
                    "AllGather", ALU.bypass, replica_groups=[list(range(NC))],
                    ins=[wut_in.ap().opt()], outs=[wut_all.ap().opt()]))
                cc.inc(g.collective_compute(
                    "AllGather", ALU.bypass, replica_groups=[list(range(NC))],
                    ins=[wdt_in.ap().opt()], outs=[wdt_all.ap().opt()]))
            blk.gpsimd(_gp)
        cc_w = cc.n
        s1b.close()

        sc_open("attn0")
        with ExitStack() as sa:
            EA = sa.enter_context
            k2g = EA(nc.sbuf_tensor("k2g", [128, FULLT], BF16))
            vg = EA(nc.sbuf_tensor("vg", [128, 32 * 64], BF16))
            pt0 = EA(nc.sbuf_tensor("pt0", [128, 3 * 1024], BF16))
            psS = EA(nc.psum_tensor("psS", [128, 2048], F32))
            psO = EA(nc.psum_tensor("psO", [128, 512], F32))
            psD = EA(nc.psum_tensor("psD", [128, 512], F32))
            with nc.Block() as blk:
                pe0, ac0, dv0, ld0 = pe.n, ac.n, dv.n, ld.n

                def _sync(sync):
                    sync.wait_ge(cc.h, cc_kv)
                    for g8 in range(HKV):
                        sync.wait_ge(pe.h, pe0 + 384 * g8)
                        for r in range(4):
                            ld.inc(sync.dma_start(
                                out=k2g[0:64, r * TOK:(r + 1) * TOK],
                                in_=k0_all[r * 512 + g8 * 64: r * 512 + (g8 + 1) * 64, :]), 16)
                            ld.inc(sync.dma_start(
                                out=k2g[64:128, r * TOK:(r + 1) * TOK],
                                in_=k0_all[r * 512 + g8 * 64: r * 512 + (g8 + 1) * 64, :]), 16)
                            for tt in range(TT8):
                                ld.inc(sync.dma_start(
                                    out=vg[:, (r * TT8 + tt) * 64:(r * TT8 + tt + 1) * 64],
                                    in_=v0_all[(r * TT8 + tt) * 128:(r * TT8 + tt + 1) * 128,
                                               g8 * 64:(g8 + 1) * 64]), 16)
                blk.sync(_sync)

                def _t(t):
                    def emit_pv(it, kt):
                        eidx = it * 32 + kt
                        t.wait_ge(ac.h, ac0 + eidx + 1)
                        if kt == 0:
                            t.wait_ge(dv.h, dv0 + it)
                        slot = eidx % 3
                        t.matmul(psO[0:64, :], vg[:, kt * 64:(kt + 1) * 64],
                                 pt0[:, slot * 1024: slot * 1024 + 512],
                                 start=(kt == 0), stop=(kt == 31))
                        t.matmul(psO[64:128, :], vg[:, kt * 64:(kt + 1) * 64],
                                 pt0[:, slot * 1024 + 512: slot * 1024 + 1024],
                                 start=(kt == 0), stop=(kt == 31), tile_position=(0, 64))
                        t.matmul(psD[0:64, :], ones_bf[:, 0:64],
                                 pt0[:, slot * 1024: slot * 1024 + 512],
                                 start=(kt == 0), stop=(kt == 31))
                        t.matmul(psD[64:128, :], ones_bf[:, 0:64],
                                 pt0[:, slot * 1024 + 512: slot * 1024 + 1024],
                                 start=(kt == 0), stop=(kt == 31),
                                 tile_position=(0, 64)).then_inc(pe.h, 1)

                    for g8 in range(HKV):
                        t.wait_ge(ld.h, ld0 + 640 * (g8 + 1))
                        for pq in range(2):
                            pair = 2 * g8 + pq
                            for qh in range(2):
                                it = pair * 2 + qh
                                for kt in range(32):
                                    eidx = it * 32 + kt
                                    if eidx >= 1:
                                        t.wait_ge(ac.h, ac0 + eidx - 1)
                                    buf = kt % 2
                                    t.matmul(psS[:, buf * 1024: buf * 1024 + 512],
                                             k2g[0:64, kt * 128:(kt + 1) * 128],
                                             q_fm[0:64, pair * TOK + qh * 512: pair * TOK + (qh + 1) * 512],
                                             start=True, stop=True).then_inc(pe.h, 1)
                                    t.matmul(psS[:, buf * 1024 + 512: buf * 1024 + 1024],
                                             k2g[64:128, kt * 128:(kt + 1) * 128],
                                             q_fm[64:128, pair * TOK + qh * 512: pair * TOK + (qh + 1) * 512],
                                             start=True, stop=True).then_inc(pe.h, 1)
                                    if kt >= 1:
                                        emit_pv(it, kt - 1)
                                emit_pv(it, 31)
                    pe.n += 32 * 96
                blk.tensor(_t)

                def _s(s):
                    for it in range(32):
                        for kt in range(32):
                            eidx = it * 32 + kt
                            buf = kt % 2
                            slot = eidx % 3
                            s.wait_ge(pe.h, pe0 + it * 96 + (3 * kt + 1 if kt >= 1 else 2))
                            s.activation(out=pt0[:, slot * 1024:(slot + 1) * 1024],
                                         in_=psS[:, buf * 1024:(buf + 1) * 1024],
                                         func=AF.Exp, scale=0.125).then_inc(ac.h, 1)
                    ac.n += 32 * 32
                blk.scalar(_s)

                def _v(v):
                    for it in range(32):
                        pair, qh = divmod(it, 2)
                        v.wait_ge(pe.h, pe0 + (it + 1) * 96)
                        v.reciprocal(out=rcp_sb[:, 0:512], in_=psD[:])
                        v.tensor_tensor(
                            out=h_bf[0:64, pair * TOK + qh * 512: pair * TOK + (qh + 1) * 512],
                            in0=psO[0:64, :], in1=rcp_sb[0:64, 0:512], op=ALU.mult)
                        v.tensor_tensor(
                            out=h_bf[64:128, pair * TOK + qh * 512: pair * TOK + (qh + 1) * 512],
                            in0=psO[64:128, :], in1=rcp_sb[64:128, 0:512], op=ALU.mult).then_inc(dv.h, 1)
                    dv.n += 32
                blk.vector(_v)

        sc_open("wo0")
        with (nc.sbuf_tensor("wf32o", [128, 2 * DC * 128], F32) as wf32o,
              nc.sbuf_tensor("wbufo", [128, 2 * DC * 128], BF16) as wbufo,
              nc.psum_tensor("psPo", [128, 2048], F32) as psP):
            def evwo(v, ot, qc, pss):
                return v.tensor_tensor(
                    out=x_fm[:, ot * TOK + qc * 512: ot * TOK + (qc + 1) * 512],
                    in0=x_fm[:, ot * TOK + qc * 512: ot * TOK + (qc + 1) * 512],
                    in1=pss, op=ALU.add)
            emit_proj(wo0, D, wf32o, wbufo, psP, evwo)

    if stages >= 2:
        with ExitStack() as s2:
            E2 = s2.enter_context
            inter_sb = E2(nc.sbuf_tensor("inter_sb", [128, OT43 * 512], BF16))
            wgu = E2(nc.sbuf_tensor("wgu", [128, 2 * 2 * DC * 128], BF16))
            wdd = E2(nc.sbuf_tensor("wdd", [128, 2 * OT43 * 128], BF16))
            gstage = E2(nc.sbuf_tensor("gstage", [128, 2 * 512], BF16))

            sc_open("ln2")
            with (nc.psum_tensor("psA2", [128, 1024], F32) as psA,
                  nc.psum_tensor("psB2", [128, 1024], F32) as psB):
                emit_ln(psA, psB, inter_sb[:, 0:DC * TOK], h_bf[:, 0:DC * TOK])

            sc_open("ffn0")
            for qh in range(2):
                with (nc.psum_tensor(f"psG{qh}", [128, 1024], F32) as psG,
                      nc.psum_tensor(f"psU{qh}", [128, 1024], F32) as psU):
                    with nc.Block() as blk:
                        wld0, pe0, ac0, dv0 = wld.n, pe.n, ac.n, dv.n

                        def _sync(sync):
                            if qh == 0:
                                sync.wait_ge(cc.h, cc_w)
                            for ot in range(OT43):
                                r, lo = divmod(ot, 6)
                                if ot >= 2:
                                    sync.wait_ge(pe.h, pe0 + 2 * (ot - 1))
                                wld.inc(sync.dma_start(
                                    out=wgu[:, (ot % 2) * 4096: (ot % 2) * 4096 + 2048],
                                    in_=bass.AP(wgt_all.ap().tensor,
                                                wgt_all.ap().offset + (r * D) * GSH + lo * 128,
                                                [[GSH, 128], [128 * GSH, DC], [1, 128]])), 16)
                                wld.inc(sync.dma_start(
                                    out=wgu[:, (ot % 2) * 4096 + 2048: (ot % 2 + 1) * 4096],
                                    in_=bass.AP(wut_all.ap().tensor,
                                                wut_all.ap().offset + (r * D) * GSH + lo * 128,
                                                [[GSH, 128], [128 * GSH, DC], [1, 128]])), 16)
                        blk.sync(_sync)

                        def _t(t):
                            for ot in range(OT43):
                                t.wait_ge(wld.h, wld0 + 32 * (ot + 1))
                                pb = ot % 2
                                if ot >= 2:
                                    t.wait_ge(dv.h, dv0 + (ot - 2) + 1)
                                    t.wait_ge(ac.h, ac0 + (ot - 2) + 1)
                                for c in range(DC):
                                    i = t.matmul(psG[:, pb * 512:(pb + 1) * 512],
                                                 wgu[:, (ot % 2) * 4096 + c * 128:(ot % 2) * 4096 + (c + 1) * 128],
                                                 h_bf[:, c * TOK + qh * 512: c * TOK + (qh + 1) * 512],
                                                 start=(c == 0), stop=(c == DC - 1))
                                pe.inc(i)
                                for c in range(DC):
                                    i = t.matmul(psU[:, pb * 512:(pb + 1) * 512],
                                                 wgu[:, (ot % 2) * 4096 + 2048 + c * 128:(ot % 2) * 4096 + 2048 + (c + 1) * 128],
                                                 h_bf[:, c * TOK + qh * 512: c * TOK + (qh + 1) * 512],
                                                 start=(c == 0), stop=(c == DC - 1))
                                pe.inc(i)
                        blk.tensor(_t)

                        def _s(s):
                            for ot in range(OT43):
                                pb = ot % 2
                                s.wait_ge(pe.h, pe0 + 2 * ot + 1)
                                if ot >= 2:
                                    s.wait_ge(dv.h, dv0 + (ot - 2) + 1)
                                s.activation(out=gstage[:, pb * 512:(pb + 1) * 512],
                                             in_=psG[:, pb * 512:(pb + 1) * 512],
                                             func=AF.Silu).then_inc(ac.h, 1)
                            ac.n += OT43
                        blk.scalar(_s)

                        def _v(v):
                            for ot in range(OT43):
                                pb = ot % 2
                                v.wait_ge(pe.h, pe0 + 2 * ot + 2)
                                v.wait_ge(ac.h, ac0 + ot + 1)
                                v.tensor_tensor(out=inter_sb[:, ot * 512:(ot + 1) * 512],
                                                in0=gstage[:, pb * 512:(pb + 1) * 512],
                                                in1=psU[:, pb * 512:(pb + 1) * 512],
                                                op=ALU.mult).then_inc(dv.h, 1)
                            dv.n += OT43
                        blk.vector(_v)

                with nc.psum_tensor(f"psDn{qh}", [128, 1024], F32) as psDn:
                    with nc.Block() as blk:
                        wld0, pe0, dv0 = wld.n, pe.n, dv.n

                        def _sync(sync):
                            for o2 in range(DC):
                                r2, lo2 = divmod(o2 * 128, DSH)
                                if o2 >= 2:
                                    sync.wait_ge(pe.h, pe0 + (o2 - 1))
                                wld.inc(sync.dma_start(
                                    out=wdd[:, (o2 % 2) * OT43 * 128:(o2 % 2 + 1) * OT43 * 128],
                                    in_=bass.AP(wdt_all.ap().tensor,
                                                wdt_all.ap().offset + r2 * INTER * DSH + lo2,
                                                [[DSH, 128], [128 * DSH, OT43], [1, 128]])), 16)
                        blk.sync(_sync)

                        def _t(t):
                            for o2 in range(DC):
                                t.wait_ge(wld.h, wld0 + 16 * (o2 + 1))
                                pb = o2 % 2
                                if o2 >= 2:
                                    t.wait_ge(dv.h, dv0 + (o2 - 2) + 1)
                                for hc in range(OT43):
                                    i = t.matmul(psDn[:, pb * 512:(pb + 1) * 512],
                                                 wdd[:, (o2 % 2) * OT43 * 128 + hc * 128:(o2 % 2) * OT43 * 128 + (hc + 1) * 128],
                                                 inter_sb[:, hc * 512:(hc + 1) * 512],
                                                 start=(hc == 0), stop=(hc == OT43 - 1))
                                pe.inc(i)
                        blk.tensor(_t)

                        def _v(v):
                            for o2 in range(DC):
                                pb = o2 % 2
                                v.wait_ge(pe.h, pe0 + o2 + 1)
                                v.tensor_tensor(
                                    out=x_fm[:, o2 * TOK + qh * 512: o2 * TOK + (qh + 1) * 512],
                                    in0=x_fm[:, o2 * TOK + qh * 512: o2 * TOK + (qh + 1) * 512],
                                    in1=psDn[:, pb * 512:(pb + 1) * 512],
                                    op=ALU.add).then_inc(dv.h, 1)
                            dv.n += DC
                        blk.vector(_v)

    if stages >= 3:
        with ExitStack() as s3:
            E3 = s3.enter_context
            qf_bf = E3(nc.sbuf_tensor("qf_bf", [65, TOK], BF16))
            m8_bf = E3(nc.sbuf_tensor("m8_bf", [128, 8], BF16))
            mcols = E3(nc.sbuf_tensor("mcols", [128, 64], F32))
            mrow_sb = E3(nc.sbuf_tensor("mrow_sb", [1, TOK], BF16))
            kfsh = E3(nc.sbuf_tensor("kfsh", [64, TOK], BF16))
            s3b = s3.enter_context(ExitStack())
            E3b = s3b.enter_context
            v1sh = E3b(nc.sbuf_tensor("v1sh", [128, TT8 * D], BF16))
            wf3 = E3b(nc.sbuf_tensor("wf3", [128, 2 * DC * 128], F32))
            wb3 = E3b(nc.sbuf_tensor("wb3", [128, 2 * DC * 128], BF16))

            sc_open("ln3")
            with (nc.psum_tensor("psA3", [128, 1024], F32) as psA,
                  nc.psum_tensor("psB3", [128, 1024], F32) as psB):
                emit_ln(psA, psB, v1sh[:, 0:DC * TOK], h_bf[:, 0:DC * TOK])

            sc_open("proj1")
            with nc.psum_tensor("psQf", [64, 2 * 1024], F32) as psQf:
                with nc.Block() as blk:
                    wld0, gp0, pe0 = wld.n, gp.n, pe.n

                    def _sync(sync):
                        for m, w_ap in enumerate((wq1, wk1)):
                            wld.inc(sync.dma_start(
                                out=wf3[:, m * (DC * 64):(m + 1) * (DC * 64)],
                                in_=bass.AP(w_ap.tensor, w_ap.offset,
                                            [[64, 128], [128 * 64, DC], [1, 64]])), 16)
                    blk.sync(_sync)

                    def _gp(g):
                        for m in range(2):
                            g.wait_ge(wld.h, wld0 + 16 * (m + 1))
                            gp.inc(g.tensor_copy(out=wb3[:, m * 1024:(m + 1) * 1024],
                                                 in_=wf3[:, m * 1024:(m + 1) * 1024]))
                    blk.gpsimd(_gp)

                    def _t(t):
                        for m in range(2):
                            t.wait_ge(gp.h, gp0 + m + 1)
                            for qc in range(2):
                                for c in range(DC):
                                    i = t.matmul(
                                        psQf[:, m * 1024 + qc * 512: m * 1024 + (qc + 1) * 512],
                                        wb3[:, m * 1024 + c * 64: m * 1024 + (c + 1) * 64],
                                        h_bf[:, c * TOK + qc * 512: c * TOK + (qc + 1) * 512],
                                        start=(c == 0), stop=(c == DC - 1))
                                pe.inc(i)
                    blk.tensor(_t)

                    def _v(v):
                        v.wait_ge(pe.h, pe0 + 4)
                        v.tensor_copy(out=qf_bf[0:64, :], in_=psQf[:, 0:1024])
                        dv.inc(v.tensor_copy(out=kfsh[:], in_=psQf[:, 1024:2048]))
                    blk.vector(_v)

            with nc.psum_tensor("psV1", [128, TT8 * 512], F32) as psV1:
                for oc in range(4):
                    with nc.Block() as blk:
                        wld0, gp0, pe0 = wld.n, gp.n, pe.n

                        def _sync(sync, oc=oc):
                            for q4 in range(4):
                                if q4 >= 2:
                                    sync.wait_ge(pe.h, pe0 + q4 - 1)
                                wld.inc(sync.dma_start(
                                    out=wf3[:, (q4 % 2) * 2048:(q4 % 2 + 1) * 2048],
                                    in_=bass.AP(wv1.tensor,
                                                wv1.offset + q4 * 4 * 128 * D + oc * 512,
                                                [[D, 128], [128 * D, 4], [1, 512]])), 16)
                        blk.sync(_sync)

                        def _gp(g):
                            for q4 in range(4):
                                g.wait_ge(wld.h, wld0 + 16 * (q4 + 1))
                                if q4 >= 2:
                                    g.wait_ge(pe.h, pe0 + q4 - 1)
                                gp.inc(g.tensor_copy(
                                    out=wb3[:, (q4 % 2) * 2048:(q4 % 2 + 1) * 2048],
                                    in_=wf3[:, (q4 % 2) * 2048:(q4 % 2 + 1) * 2048]))
                        blk.gpsimd(_gp)

                        def _t(t):
                            for q4 in range(4):
                                t.wait_ge(gp.h, gp0 + q4 + 1)
                                for tt in range(TT8):
                                    for c4 in range(4):
                                        c = q4 * 4 + c4
                                        i = t.matmul(
                                            psV1[:, tt * 512:(tt + 1) * 512],
                                            h_bf[:, c * TOK + tt * 128: c * TOK + (tt + 1) * 128],
                                            wb3[:, (q4 % 2) * 2048 + c4 * 512:(q4 % 2) * 2048 + (c4 + 1) * 512],
                                            start=(c == 0), stop=(c == DC - 1))
                                pe.inc(i)
                        blk.tensor(_t)

                        def _v(v, oc=oc):
                            v.wait_ge(pe.h, pe0 + 4)
                            for tt in range(TT8):
                                dv.inc(v.tensor_copy(
                                    out=v1sh[:, tt * D + oc * 512: tt * D + (oc + 1) * 512],
                                    in_=psV1[:, tt * 512:(tt + 1) * 512]))
                        blk.vector(_v)

            sc_open("kvship1")
            with nc.Block() as blk:
                def _gp(g):
                    gld.inc(g.dma_start(out=kf_in[:, :], in_=kfsh[:]), 16)
                    for tt in range(TT8):
                        gld.inc(g.dma_start(out=v1_in[tt * 128:(tt + 1) * 128, :],
                                            in_=v1sh[:, tt * D:(tt + 1) * D]), 16)
                    g.wait_ge(gld.h, gld.n)
                    cc.inc(g.collective_compute(
                        "AllGather", ALU.bypass,
                        replica_groups=[[0, 1, 2, 3], [4, 5, 6, 7]],
                        ins=[kf_in.ap().opt()], outs=[kf_all.ap().opt()]))
                    cc.inc(g.collective_compute(
                        "AllGather", ALU.bypass,
                        replica_groups=[[0, 1, 2, 3], [4, 5, 6, 7]],
                        ins=[v1_in.ap().opt()], outs=[v1_all.ap().opt()]))
                blk.gpsimd(_gp)
            cc_kv1 = cc.n
            s3b.close()

            with ExitStack() as sa3:
                EA3 = sa3.enter_context
                kf_full = EA3(nc.sbuf_tensor("kf_full", [65, FULLT], BF16))
                pt1 = EA3(nc.sbuf_tensor("pt1", [128, 32 * 512], BF16))
                vt_buf = EA3(nc.sbuf_tensor("vt_buf", [128, 2 * 4096], BF16))
                psS1 = EA3(nc.psum_tensor("psS1", [128, 2 * 512], F32))
                psDen1 = EA3(nc.psum_tensor("psDen1", [128, 512], F32))
                psO3 = EA3(nc.psum_tensor("psO3", [128, 2 * 512], F32))

                sc_open("mstab")
                with nc.psum_tensor("psMt", [128, 1536], F32) as psMt:
                    with nc.Block() as blk:
                        ld0m, pe0m, dv0m = ld.n, pe.n, dv.n

                        def _sync(sync):
                            sync.wait_ge(cc.h, cc_kv1)
                            for r in range(4):
                                ld.inc(sync.dma_start(
                                    out=kf_full[0:64, r * TOK:(r + 1) * TOK],
                                    in_=kf_all[r * 64:(r + 1) * 64, :]), 16)
                        blk.sync(_sync)
                        ldm = ld.n

                        def _v0(v):
                            v.wait_ge(ld.h, ldm)
                            dv.inc(v.memset(kf_full[64:65, :], 1.0))
                            for i in range(64):
                                v.wait_ge(pe.h, pe0m + i + 1)
                                dv.inc(v.reduce_max(out=mcols[:, i:i + 1],
                                                    in_=psMt[:, (i % 3) * 512:(i % 3 + 1) * 512],
                                                    axis=mybir.AxisListType.X))
                            for qt in range(TT8):
                                i = v.reduce_max(out=tmp_sb[:, qt:qt + 1],
                                                 in_=mcols[:, qt * 8:(qt + 1) * 8],
                                                 axis=mybir.AxisListType.X)
                            dv.inc(v.tensor_copy(out=m8_bf[:], in_=tmp_sb[:, 0:8]))
                        blk.vector(_v0)

                        def _t0(t):
                            t.wait_ge(dv.h, dv0m + 1)
                            for i in range(64):
                                qt, kc = divmod(i, 8)
                                if i >= 3:
                                    t.wait_ge(dv.h, dv0m + i - 1)
                                t.matmul(psMt[:, (i % 3) * 512:(i % 3 + 1) * 512],
                                         qf_bf[0:64, qt * 128:(qt + 1) * 128],
                                         kf_full[0:64, kc * 512:(kc + 1) * 512],
                                         start=True, stop=True).then_inc(pe.h, 1)
                            pe.n += 64
                        blk.tensor(_t0)
                with (nc.psum_tensor("psMrow", [1, 1024], F32) as psMrow,
                      nc.Block() as blk):
                    pe0m, dv0m = pe.n, dv.n

                    def _t0(t):
                        for qt in range(TT8):
                            i = t.matmul(psMrow[0:1, qt * 128:(qt + 1) * 128],
                                         m8_bf[:, qt:qt + 1], ident_bf[:, :],
                                         start=True, stop=True)
                        pe.inc(i)
                    blk.tensor(_t0)

                    def _v0(v):
                        v.wait_ge(pe.h, pe0m + 1)
                        dv.inc(v.tensor_scalar(out=mrow_sb[:], in0=psMrow[0:1, :],
                                               scalar1=-1.0, scalar2=None, op0=ALU.mult))
                    blk.vector(_v0)

                    def _g0(g):
                        g.wait_ge(dv.h, dv0m + 1)
                        gld.inc(g.dma_start(out=qf_bf[64:65, :], in_=mrow_sb[0:1, :]), 16)
                        g.wait_ge(gld.h, gld.n)
                    blk.gpsimd(_g0)

                sc_open("attn1")
                for qh in range(2):
                    with nc.Block() as blk:
                        ld0, pe0, ac0, dv0 = ld.n, pe.n, ac.n, dv.n

                        def _sync(sync, qh=qh):
                            pass
                        blk.sync(_sync)
                        ldt = ld.n

                        def _t(t, qh=qh):
                            t.wait_ge(ld.h, ldt)
                            def emit_den(kt):
                                t.wait_ge(ac.h, ac0 + kt + 1)
                                t.matmul(psDen1[:], ones_bf[:],
                                         pt1[:, kt * 512:(kt + 1) * 512],
                                         start=(kt == 0), stop=(kt == 31)).then_inc(pe.h, 1)
                            for kt in range(32):
                                if kt >= 2:
                                    t.wait_ge(ac.h, ac0 + kt - 1)
                                t.matmul(psS1[:, (kt % 2) * 512:(kt % 2 + 1) * 512],
                                         kf_full[0:65, kt * 128:(kt + 1) * 128],
                                         qf_bf[0:65, qh * 512:(qh + 1) * 512],
                                         start=True, stop=True).then_inc(pe.h, 1)
                                if kt >= 1:
                                    emit_den(kt - 1)
                            emit_den(31)
                            pe.n += 64
                        blk.tensor(_t)

                        def _s(s):
                            for kt in range(32):
                                s.wait_ge(pe.h, pe0 + (2 * kt if kt >= 1 else 1))
                                s.activation(out=pt1[:, kt * 512:(kt + 1) * 512],
                                             in_=psS1[:, (kt % 2) * 512:(kt % 2 + 1) * 512],
                                             func=AF.Exp).then_inc(ac.h, 1)
                            ac.n += 32
                        blk.scalar(_s)

                        def _v(v, qh=qh):
                            v.wait_ge(pe.h, pe0 + 64)
                            dv.inc(v.reciprocal(out=rcp_sb[:, qh * 512:(qh + 1) * 512],
                                                in_=psDen1[:]))
                        blk.vector(_v)

                    with nc.Block() as blk:
                        wld0, pe0, dv0 = wld.n, pe.n, dv.n

                        def _sync(sync):
                            for oc in range(DC):
                                if oc >= 2:
                                    sync.wait_ge(pe.h, pe0 + oc - 1)
                                for r in range(4):
                                    wld.inc(sync.dma_start(
                                        out=vt_buf[:, (oc % 2) * 4096 + r * 1024:(oc % 2) * 4096 + (r + 1) * 1024],
                                        in_=bass.AP(v1_all.ap().tensor,
                                                    v1_all.ap().offset + r * TOK * D + oc * 128,
                                                    [[D, 128], [128 * D, TT8], [1, 128]])), 16)
                        blk.sync(_sync)

                        def _t(t):
                            for oc in range(DC):
                                t.wait_ge(wld.h, wld0 + 64 * (oc + 1))
                                if oc >= 2:
                                    t.wait_ge(dv.h, dv0 + oc - 1)
                                for kt in range(32):
                                    i = t.matmul(
                                        psO3[:, (oc % 2) * 512:(oc % 2 + 1) * 512],
                                        vt_buf[:, (oc % 2) * 4096 + kt * 128:(oc % 2) * 4096 + (kt + 1) * 128],
                                        pt1[:, kt * 512:(kt + 1) * 512],
                                        start=(kt == 0), stop=(kt == 31))
                                pe.inc(i)
                        blk.tensor(_t)

                        def _v(v, qh=qh):
                            for oc in range(DC):
                                v.wait_ge(pe.h, pe0 + oc + 1)
                                v.tensor_tensor(
                                    out=h_bf[:, oc * TOK + qh * 512: oc * TOK + (qh + 1) * 512],
                                    in0=psO3[:, (oc % 2) * 512:(oc % 2 + 1) * 512],
                                    in1=rcp_sb[:, qh * 512:(qh + 1) * 512],
                                    op=ALU.mult).then_inc(dv.h, 1)
                            dv.n += DC
                        blk.vector(_v)

            sc_open("wo1")
            with (nc.sbuf_tensor("wf3o", [128, 2 * DC * 128], F32) as wf3o,
                  nc.sbuf_tensor("wb3o", [128, 2 * DC * 128], BF16) as wb3o,
                  nc.psum_tensor("psP3", [128, 2048], F32) as psP3):
                def evwo1(v, ot, qc, pss):
                    return v.tensor_tensor(
                        out=x_fm[:, ot * TOK + qc * 512: ot * TOK + (qc + 1) * 512],
                        in0=x_fm[:, ot * TOK + qc * 512: ot * TOK + (qc + 1) * 512],
                        in1=pss, op=ALU.add)
                emit_proj(wo1, D, wf3o, wb3o, psP3, evwo1)

    if stages >= 4:
        with ExitStack() as s4:
            E4 = s4.enter_context
            r_sb = E4(nc.sbuf_tensor("r_sb", [128, DC * TOK], BF16))
            wf4 = E4(nc.sbuf_tensor("wf4", [128, 2 * DC * 128], F32))
            wb4 = E4(nc.sbuf_tensor("wb4", [128, 2 * DC * 128], BF16))
            lg_bf = E4(nc.sbuf_tensor("lg_bf", [8, TOK], BF16))
            lt_f = E4(nc.sbuf_tensor("lt_f", [128, 64], F32))
            m1_sb = E4(nc.sbuf_tensor("m1_sb", [128, 8], F32))
            md_sb = E4(nc.sbuf_tensor("md_sb", [128, 8], F32))
            ga_sb = E4(nc.sbuf_tensor("ga_sb", [128, 64], F32))
            gb_sb = E4(nc.sbuf_tensor("gb_sb", [128, 64], F32))
            e_sb = E4(nc.sbuf_tensor("e_sb", [128, 8], F32))
            S_bf = E4(nc.sbuf_tensor("S_bf", [128, 8], BF16))
            srow_bf = E4(nc.sbuf_tensor("srow_bf", [1, TOK], BF16))
            srep_bf = E4(nc.sbuf_tensor("srep_bf", [128, TOK], BF16))

            sc_open("ln4")
            with (nc.psum_tensor("psA4", [128, 1024], F32) as psA,
                  nc.psum_tensor("psB4", [128, 1024], F32) as psB):
                emit_ln(psA, psB, r_sb[:, 0:DC * TOK], h_bf[:, 0:DC * TOK])

            sc_open("gate4")
            with (nc.psum_tensor("psL", [8, 1024], F32) as psL,
                  nc.psum_tensor("psT4", [128, 64], BF16) as psT4,
                  nc.psum_tensor("psRow", [1, 1024], F32) as psRow,
                  nc.psum_tensor("psSrep", [128, 1024], F32) as psSrep):
                with nc.Block() as blk:
                    wld0, gp0, pe0, dv0, ac0 = wld.n, gp.n, pe.n, dv.n, ac.n

                    def _sync(sync):
                        wld.inc(sync.dma_start(
                            out=wf4[:, 0:DC * 8],
                            in_=bass.AP(wg8.tensor, wg8.offset,
                                        [[8, 128], [128 * 8, DC], [1, 8]])), 16)
                    blk.sync(_sync)

                    def _gp(g):
                        g.wait_ge(wld.h, wld0 + 16)
                        gp.inc(g.tensor_copy(out=wb4[:, 0:DC * 8], in_=wf4[:, 0:DC * 8]))
                    blk.gpsimd(_gp)

                    def _t(t):
                        t.wait_ge(gp.h, gp0 + 1)
                        for qc in range(2):
                            for c in range(DC):
                                i = t.matmul(psL[:, qc * 512:(qc + 1) * 512],
                                             wb4[:, c * 8:(c + 1) * 8],
                                             h_bf[:, c * TOK + qc * 512: c * TOK + (qc + 1) * 512],
                                             start=(c == 0), stop=(c == DC - 1))
                            pe.inc(i)
                        t.wait_ge(dv.h, dv0 + 1)
                        for j in range(8):
                            i = t.transpose(psT4[:, j * 8:(j + 1) * 8],
                                            lg_bf[0:8, j * 128:(j + 1) * 128],
                                            ident_bf[0:8, 0:8])
                        pe.inc(i)
                        t.wait_ge(dv.h, dv0 + 3)
                        for j in range(8):
                            i = t.matmul(psRow[0:1, j * 128:(j + 1) * 128],
                                         S_bf[:, j:j + 1], ident_bf[:, :],
                                         start=True, stop=True)
                        pe.inc(i)
                        t.wait_ge(dv.h, dv0 + 4)
                        for qc in range(2):
                            i = t.matmul(psSrep[:, qc * 512:(qc + 1) * 512],
                                         ones_bf[0:1, :], srow_bf[0:1, qc * 512:(qc + 1) * 512],
                                         start=True, stop=True)
                        pe.inc(i)
                    blk.tensor(_t)

                    def _v(v):
                        v.wait_ge(pe.h, pe0 + 2)
                        dv.inc(v.tensor_copy(out=lg_bf[:], in_=psL[:, 0:1024]))
                        v.wait_ge(pe.h, pe0 + 3)
                        v.tensor_copy(out=lt_f[:], in_=psT4[:])
                        for j in range(8):
                            v.reduce_max(out=m1_sb[:, j:j + 1], in_=lt_f[:, j * 8:(j + 1) * 8],
                                         axis=mybir.AxisListType.X)
                        v.tensor_copy(out=ga_sb[:], in_=lt_f[:])
                        v.tensor_copy(out=gb_sb[:], in_=lt_f[:])
                        for j in range(8):
                            v.tensor_scalar_sub(out=ga_sb[:, j * 8:(j + 1) * 8],
                                                in0=lt_f[:, j * 8:(j + 1) * 8],
                                                scalar1=m1_sb[:, j:j + 1])
                        v.tensor_scalar(out=gb_sb[:], in0=ga_sb[:], scalar1=0.0,
                                        scalar2=-1.0e30, op0=ALU.is_equal, op1=ALU.mult)
                        v.tensor_tensor(out=gb_sb[:], in0=gb_sb[:], in1=ga_sb[:], op=ALU.add)
                        for j in range(8):
                            i = v.reduce_max(out=md_sb[:, j:j + 1], in_=gb_sb[:, j * 8:(j + 1) * 8],
                                             axis=mybir.AxisListType.X)
                        dv.inc(i)
                        v.wait_ge(ac.h, ac0 + 2)
                        dv.inc(v.tensor_tensor(out=S_bf[:], in0=e_sb[:], in1=ga_sb[:, 0:8],
                                               op=ALU.add))
                        v.wait_ge(pe.h, pe0 + 4)
                        dv.inc(v.tensor_copy(out=srow_bf[:], in_=psRow[0:1, :]))
                        v.wait_ge(pe.h, pe0 + 5)
                        dv.inc(v.tensor_copy(out=srep_bf[:], in_=psSrep[:, 0:1024]))
                    blk.vector(_v)

                    def _s(s):
                        s.wait_ge(dv.h, dv0 + 2)
                        ac.inc(s.activation(out=e_sb[:], in_=md_sb[:], func=AF.Sigmoid))
                        ac.inc(s.activation(out=ga_sb[:, 0:8], in_=md_sb[:], func=AF.Sigmoid,
                                            scale=-1.0))
                    blk.scalar(_s)

            sc_open("updown")
            with nc.psum_tensor("psP4", [128, 2048], F32) as psP4:
                def evrelu(v, ot, qc, pss):
                    return v.tensor_scalar_max(
                        out=r_sb[:, ot * TOK + qc * 512: ot * TOK + (qc + 1) * 512],
                        in0=pss, scalar1=0.0)
                emit_proj(wup, D, wf4, wb4, psP4, evrelu)

                def evdn(v, ot, qc, pss):
                    return v.tensor_tensor(
                        out=x_fm[:, ot * TOK + qc * 512: ot * TOK + (qc + 1) * 512],
                        in0=x_fm[:, ot * TOK + qc * 512: ot * TOK + (qc + 1) * 512],
                        in1=pss, op=ALU.add)
                emit_proj(wdn, D, wf4, wb4, psP4, evdn, rhs=r_sb)

            sc_open("expert")
            with nc.Block() as blk:
                def _v(v):
                    for c in range(DC):
                        i = v.tensor_tensor(out=h_bf[:, c * TOK:(c + 1) * TOK],
                                            in0=h_bf[:, c * TOK:(c + 1) * TOK],
                                            in1=srep_bf[:], op=ALU.mult)
                    dv.inc(i)
                blk.vector(_v)
            with nc.psum_tensor("psP5", [128, 2048], F32) as psP5:
                def evex(v, ot, qc, pss):
                    return v.tensor_tensor(
                        out=x_fm[:, ot * TOK + qc * 512: ot * TOK + (qc + 1) * 512],
                        in0=x_fm[:, ot * TOK + qc * 512: ot * TOK + (qc + 1) * 512],
                        in1=pss, op=ALU.add)
                emit_proj(wex, D, wf4, wb4, psP5, evex)

    sc_open("p_store")
    with nc.Block() as blk:
        def _sync(sync):
            for c in range(DC):
                st.inc(sync.dma_start(out=ys[c * 128:(c + 1) * 128, :],
                                      in_=x_fm[:, c * TOK:(c + 1) * TOK]), 16)
            sync.wait_ge(st.h, st.n)
        blk.sync(_sync)

    sc_close()
    ctx.close()
    return nc


def kernel(**inputs):
    ii = {k: np.asarray(v) for k, v in inputs.items()}
    x = ii["L0_attn_x"].astype(np.float32, copy=False)
    C = np.ascontiguousarray

    gate_idx = ii["L0_ffn_gate_idx"].astype(np.int32, copy=False)
    up_idx = ii["L0_ffn_up_idx"].astype(np.int32, copy=False)
    down_idx = ii["L0_ffn_down_idx"].astype(np.int32, copy=False)
    gpad = np.zeros((NC * GSH, D), np.int32)
    gpad[:INTER] = gate_idx
    upad = np.zeros((NC * GSH, D), np.int32)
    upad[:INTER] = up_idx

    shared = {
        "wq0": C(ii["L0_attn_Wq"].astype(np.float32, copy=False)),
        "wk0": C(ii["L0_attn_Wk"].astype(np.float32, copy=False)),
        "wv0": C(ii["L0_attn_Wv"].astype(np.float32, copy=False)),
        "wo0": C(ii["L0_attn_Wo"].astype(np.float32, copy=False)),
        "lut": C(ii["L0_ffn_lut"].astype(np.float32, copy=False)),
        "wq1": C(ii["L1_attn_Wq"].astype(np.float32, copy=False).T),
        "wk1": C(ii["L1_attn_Wk"].astype(np.float32, copy=False).T),
        "wv1": C(ii["L1_attn_Wv"].astype(np.float32, copy=False).T),
        "wo1": C(ii["L1_attn_Wo"].astype(np.float32, copy=False).T),
        "wup": C(ii["L1_ffn_W_up"].astype(np.float32, copy=False).T),
        "wdn": C(ii["L1_ffn_W_down"].astype(np.float32, copy=False).T),
        "wg8": C(ii["L1_ffn_W_gate"].astype(np.float32, copy=False).T),
        "wex": C(ii["L1_ffn_W_expert"].astype(np.float32, copy=False)),
    }
    in_maps = []
    for c in range(NC):
        b, qi = divmod(c, 4)
        m = dict(shared)
        m["xs"] = C(x[b, qi * TOK:(qi + 1) * TOK, :].T)
        m["gidx"] = C(gpad[c * GSH:(c + 1) * GSH, :].T)
        m["uidx"] = C(upad[c * GSH:(c + 1) * GSH, :].T)
        m["didx"] = C(down_idx[c * DSH:(c + 1) * DSH, :].T)
        in_maps.append(m)

    nc_prog = build_nc()
    res = run_bass_kernel_spmd(nc_prog, in_maps, core_ids=list(range(NC)))
    y = np.empty((2, FULLT, D), np.float32)
    for c in range(NC):
        b, qi = divmod(c, 4)
        y[b, qi * TOK:(qi + 1) * TOK, :] = res.results[c]["ys"].T
    return y


# version 13 (final)
# speedup vs baseline: 1.0389x; 1.0389x over previous
import numpy as np
import ml_dtypes
from contextlib import ExitStack

import concourse.bass as bass
import concourse.mybir as mybir
from concourse.bass_utils import run_bass_kernel_spmd

F32 = mybir.dt.float32
BF16 = mybir.dt.bfloat16
I32 = mybir.dt.int32
AF = mybir.ActivationFunctionType
ALU = mybir.AluOpType

NC = 8
D = 2048
TOK = 1024
DC = 16
TT8 = 8
FULLT = 4096
INTER = 5504
OT43 = 43
GSH = 768
DSH = 256
HQ, HKV, DH = 32, 8, 64
EPS = 1e-5

STAGES = 4


class Sem:
    def __init__(self, nc, ctx, name):
        self.h = ctx.enter_context(nc.semaphore(name))
        self.n = 0

    def inc(self, instr, k=1):
        instr.then_inc(self.h, k)
        self.n += k
        return self.n


def build_nc(stages=None):
    if stages is None:
        stages = STAGES
    nc = bass.Bass("TRN2", target_bir_lowering=False)

    xs = nc.dram_tensor("xs", [D, TOK], F32, kind="ExternalInput").ap()
    wq0 = nc.dram_tensor("wq0", [D, D], F32, kind="ExternalInput").ap()
    wk0 = nc.dram_tensor("wk0", [D, 512], F32, kind="ExternalInput").ap()
    wv0 = nc.dram_tensor("wv0", [D, 512], F32, kind="ExternalInput").ap()
    wo0 = nc.dram_tensor("wo0", [D, D], F32, kind="ExternalInput").ap()
    lut = nc.dram_tensor("lut", [16], F32, kind="ExternalInput").ap()
    gidx = nc.dram_tensor("gidx", [D, GSH], I32, kind="ExternalInput").ap()
    uidx = nc.dram_tensor("uidx", [D, GSH], I32, kind="ExternalInput").ap()
    didx = nc.dram_tensor("didx", [INTER, DSH], I32, kind="ExternalInput").ap()
    wq1 = nc.dram_tensor("wq1", [D, 64], F32, kind="ExternalInput").ap()
    wk1 = nc.dram_tensor("wk1", [D, 64], F32, kind="ExternalInput").ap()
    wv1 = nc.dram_tensor("wv1", [D, D], F32, kind="ExternalInput").ap()
    wo1 = nc.dram_tensor("wo1", [D, D], F32, kind="ExternalInput").ap()
    wup = nc.dram_tensor("wup", [D, D], F32, kind="ExternalInput").ap()
    wdn = nc.dram_tensor("wdn", [D, D], F32, kind="ExternalInput").ap()
    wg8 = nc.dram_tensor("wg8", [D, 8], F32, kind="ExternalInput").ap()
    wex = nc.dram_tensor("wex", [D, D], F32, kind="ExternalInput").ap()
    ys = nc.dram_tensor("ys", [D, TOK], F32, kind="ExternalOutput").ap()

    wgt_in = nc.dram_tensor("wgt_in", [D, GSH], BF16)
    wut_in = nc.dram_tensor("wut_in", [D, GSH], BF16)
    wdt_in = nc.dram_tensor("wdt_in", [INTER, DSH], BF16)
    wgt_all = nc.dram_tensor("wgt_all", [NC * D, GSH], BF16)
    wut_all = nc.dram_tensor("wut_all", [NC * D, GSH], BF16)
    wdt_all = nc.dram_tensor("wdt_all", [NC * INTER, DSH], BF16)
    k0_in = nc.dram_tensor("k0_in", [512, TOK], BF16)
    v0_in = nc.dram_tensor("v0_in", [TOK, 512], BF16)
    k0_all = nc.dram_tensor("k0_all", [4 * 512, TOK], BF16)
    v0_all = nc.dram_tensor("v0_all", [4 * TOK, 512], BF16)
    kf_in = nc.dram_tensor("kf_in", [64, TOK], BF16)
    v1_in = nc.dram_tensor("v1_in", [TOK, D], BF16)
    kf_all = nc.dram_tensor("kf_all", [4 * 64, TOK], BF16)
    v1_all = nc.dram_tensor("v1_all", [4 * TOK, D], BF16)

    ctx = ExitStack()
    E = ctx.enter_context
    _sc = [None]
    def sc_open(name):
        if _sc[0]: nc.leave_named_scope(_sc[0][0], _sc[0][1], False)
        sid, _ = nc.enter_named_scope(name, False)
        _sc[0] = (name, sid)
    def sc_close():
        if _sc[0]:
            nc.leave_named_scope(_sc[0][0], _sc[0][1], False); _sc[0] = None

    x_fm = E(nc.sbuf_tensor("x_fm", [128, DC * TOK], F32))
    h_bf = E(nc.sbuf_tensor("h_bf", [128, DC * TOK], BF16))
    ones_bf = E(nc.sbuf_tensor("ones_bf", [128, 128], BF16))
    ident_bf = E(nc.sbuf_tensor("ident_bf", [128, 128], BF16))
    lut128 = E(nc.sbuf_tensor("lut128", [128, 16], F32))
    lut_bf = E(nc.sbuf_tensor("lut_bf", [1, 16], BF16))
    lut_f = E(nc.sbuf_tensor("lut_f", [1, 16], F32))
    it_i = E(nc.sbuf_tensor("it_i", [128, 128], I32))
    ip_i = E(nc.sbuf_tensor("ip_i", [128, 1], I32))
    it_f = E(nc.sbuf_tensor("it_f", [128, 128], F32))
    ip_f = E(nc.sbuf_tensor("ip_f", [128, 1], F32))
    mean_sb = E(nc.sbuf_tensor("mean_sb", [128, TOK], F32))
    rstd_sb = E(nc.sbuf_tensor("rstd_sb", [128, TOK], F32))
    tmp_sb = E(nc.sbuf_tensor("tmp_sb", [128, TOK], F32))
    rcp_sb = E(nc.sbuf_tensor("rcp_sb", [128, TOK], F32))

    ld = Sem(nc, ctx, "ld")
    wld = Sem(nc, ctx, "wld")
    gld = Sem(nc, ctx, "gld")
    pe = Sem(nc, ctx, "pe")
    ac = Sem(nc, ctx, "ac")
    dv = Sem(nc, ctx, "dv")
    gp = Sem(nc, ctx, "gp")
    cc = Sem(nc, ctx, "cc")
    st = Sem(nc, ctx, "st")

    sc_open("p0")
    with nc.Block() as blk:
        def _sync(sync):
            for c in range(DC):
                ld.inc(sync.dma_start(out=x_fm[:, c * TOK:(c + 1) * TOK],
                                      in_=xs[c * 128:(c + 1) * 128, :]), 16)
            ld.inc(sync.dma_start(out=lut_f[:], in_=lut[None, :]), 16)
        blk.sync(_sync)

        def _gp(g):
            gp.inc(g.iota(it_i[:], pattern=[[1, 128]], base=0, channel_multiplier=0))
            gp.inc(g.iota(ip_i[:], pattern=[[0, 1]], base=0, channel_multiplier=1))
            g.wait_ge(gp.h, gp.n)
            gp.inc(g.tensor_copy(out=it_f[:], in_=it_i[:]))
            gp.inc(g.tensor_copy(out=ip_f[:], in_=ip_i[:]))
        blk.gpsimd(_gp)

        def _v(v):
            v.memset(ones_bf[:], 1.0)
            v.wait_ge(gp.h, gp.n)
            v.tensor_scalar(out=ident_bf[:], in0=it_f[:], scalar1=ip_f[:],
                            scalar2=None, op0=ALU.is_equal)
            v.wait_ge(ld.h, ld.n)
            dv.inc(v.tensor_copy(out=lut_bf[:], in_=lut_f[:]))
        blk.vector(_v)
    with (nc.psum_tensor("pslut", [128, 16], F32) as pslut, nc.Block() as blk):
        def _t(t):
            pe.inc(t.matmul(pslut[:], ones_bf[0:1, :], lut_bf[:], start=True, stop=True))
        blk.tensor(_t)

        def _s(s):
            s.wait_ge(pe.h, pe.n)
            ac.inc(s.copy(out=lut128[:], in_=pslut[:]))
        blk.scalar(_s)

    def emit_ln(psA, psB, xb, xsq):
        with nc.Block() as blk:
            def _v(v):
                dv.inc(v.tensor_copy(out=xb, in_=x_fm[:]))
            blk.vector(_v)

            def _s(s):
                ac.inc(s.activation(out=xsq, in_=x_fm[:], func=AF.Square))
            blk.scalar(_s)

            dvn, acn = dv.n, ac.n
            def _t(t):
                t.wait_ge(dv.h, dvn)
                t.wait_ge(ac.h, acn)
                for src, ps in ((xb, psA), (xsq, psB)):
                    for qc in range(2):
                        for c in range(DC):
                            i = t.matmul(ps[:, qc * 512:(qc + 1) * 512], ones_bf[:],
                                         src[:, c * TOK + qc * 512: c * TOK + (qc + 1) * 512],
                                         start=(c == 0), stop=(c == DC - 1))
                        pe.inc(i)
            blk.tensor(_t)
        with nc.Block() as blk:
            def _v(v):
                v.tensor_scalar(out=mean_sb[:], in0=psA[:, 0:TOK], scalar1=1.0 / D,
                                scalar2=None, op0=ALU.mult)
                v.tensor_scalar(out=rstd_sb[:], in0=psB[:, 0:TOK], scalar1=1.0 / D,
                                scalar2=None, op0=ALU.mult)
                v.tensor_tensor(out=tmp_sb[:], in0=mean_sb[:], in1=mean_sb[:], op=ALU.mult)
                v.tensor_tensor(out=rcp_sb[:], in0=rstd_sb[:], in1=tmp_sb[:], op=ALU.subtract)
                dv.inc(v.tensor_scalar_add(out=rstd_sb[:], in0=rcp_sb[:], scalar1=EPS))
                v.wait_ge(ac.h, ac.n + 1)
                for c in range(DC):
                    v.tensor_tensor(out=tmp_sb[:], in0=x_fm[:, c * TOK:(c + 1) * TOK],
                                    in1=mean_sb[:], op=ALU.subtract)
                    i = v.tensor_tensor(out=h_bf[:, c * TOK:(c + 1) * TOK],
                                        in0=tmp_sb[:], in1=rstd_sb[:], op=ALU.mult)
                dv.inc(i)
            blk.vector(_v)

            dvn = dv.n - 1
            def _s(s):
                s.wait_ge(dv.h, dvn)
                s.activation(out=psA[:, 0:TOK], in_=rstd_sb[:], func=AF.Ln)
                ac.inc(s.activation(out=rstd_sb[:], in_=psA[:, 0:TOK], func=AF.Exp, scale=-0.5))
            blk.scalar(_s)

    def emit_proj(w_ap, OW, wf32, wbuf, psP, evac, wait_sems=(), rhs=None):
        TW = 128
        not_ = OW // TW
        rh = rhs if rhs is not None else h_bf
        with nc.Block() as blk:
            wld0, pe0, gp0, dv0 = wld.n, pe.n, gp.n, dv.n

            def _sync(sync):
                for sh, val in wait_sems:
                    sync.wait_ge(sh, val)
                for ot in range(not_):
                    if ot >= 2:
                        sync.wait_ge(pe.h, pe0 + 2 * (ot - 1))
                    wld.inc(sync.dma_start(
                        out=wf32[:, (ot % 2) * (DC * TW):(ot % 2 + 1) * (DC * TW)],
                        in_=bass.AP(w_ap.tensor, w_ap.offset + ot * TW,
                                    [[OW, 128], [128 * OW, DC], [1, TW]])), 16)
            blk.sync(_sync)

            def _gp(g):
                for ot in range(not_):
                    g.wait_ge(wld.h, wld0 + 16 * (ot + 1))
                    if ot >= 2:
                        g.wait_ge(pe.h, pe0 + 2 * (ot - 1))
                    gp.inc(g.tensor_copy(
                        out=wbuf[:, (ot % 2) * (DC * TW):(ot % 2 + 1) * (DC * TW)],
                        in_=wf32[:, (ot % 2) * (DC * TW):(ot % 2 + 1) * (DC * TW)]))
            blk.gpsimd(_gp)

            def _t(t):
                for ot in range(not_):
                    t.wait_ge(gp.h, gp0 + ot + 1)
                    pb = ot % 2
                    if ot >= 2:
                        t.wait_ge(dv.h, dv0 + 2 * (ot - 2) + 2)
                    for qc in range(2):
                        for c in range(DC):
                            i = t.matmul(
                                psP[:, pb * 1024 + qc * 512: pb * 1024 + (qc + 1) * 512],
                                wbuf[:, (ot % 2) * (DC * TW) + c * TW:
                                     (ot % 2) * (DC * TW) + (c + 1) * TW],
                                rh[:, c * TOK + qc * 512: c * TOK + (qc + 1) * 512],
                                start=(c == 0), stop=(c == DC - 1))
                        pe.inc(i)
            blk.tensor(_t)

            def _v(v):
                for ot in range(not_):
                    pb = ot % 2
                    for qc in range(2):
                        v.wait_ge(pe.h, pe0 + 2 * ot + qc + 1)
                        dv.inc(evac(v, ot, qc,
                                    psP[:, pb * 1024 + qc * 512: pb * 1024 + (qc + 1) * 512]))
            blk.vector(_v)

    with ExitStack() as s1:
        E1 = s1.enter_context
        q_fm = E1(nc.sbuf_tensor("q_fm", [128, DC * TOK], BF16))
        s1b = s1.enter_context(ExitStack())
        E1b = s1b.enter_context
        wf32 = E1b(nc.sbuf_tensor("wf32", [128, 2 * DC * 128], F32))
        wbuf = E1b(nc.sbuf_tensor("wbuf", [128, 2 * DC * 128], BF16))
        ksh = E1b(nc.sbuf_tensor("ksh", [128, 4 * TOK], BF16))
        vsh = E1b(nc.sbuf_tensor("vsh", [128, 8 * 512], BF16))
        midx = E1b(nc.sbuf_tensor("midx", [128, GSH], I32))
        mbf = E1b(nc.sbuf_tensor("mbf", [128, GSH], BF16))
        mterm = E1b(nc.sbuf_tensor("mterm", [128, GSH], BF16))
        macc = E1b(nc.sbuf_tensor("macc", [128, 2 * GSH], BF16))

        sc_open("ln1")
        with (nc.psum_tensor("psA", [128, 1024], F32) as psA,
              nc.psum_tensor("psB", [128, 1024], F32) as psB):
            emit_ln(psA, psB, q_fm[:, 0:DC * TOK], h_bf[:, 0:DC * TOK])

        sc_open("qkv0")
        with nc.psum_tensor("psP", [128, 2048], F32) as psP:
            def evq(v, ot, qc, pss):
                return v.tensor_copy(out=q_fm[:, ot * TOK + qc * 512: ot * TOK + (qc + 1) * 512],
                                     in_=pss)
            emit_proj(wq0, D, wf32, wbuf, psP, evq)

            def evk(v, ot, qc, pss):
                return v.tensor_copy(out=ksh[:, ot * TOK + qc * 512: ot * TOK + (qc + 1) * 512],
                                     in_=pss)
            emit_proj(wk0, 512, wf32, wbuf, psP, evk)

        with nc.Block() as blk:
            wld0, pe0, gp0 = wld.n, pe.n, gp.n

            def _sync(sync):
                for q4 in range(4):
                    if q4 >= 2:
                        sync.wait_ge(pe.h, pe0 + q4 - 1)
                    wld.inc(sync.dma_start(
                        out=wf32[:, (q4 % 2) * 2048:(q4 % 2 + 1) * 2048],
                        in_=bass.AP(wv0.tensor, wv0.offset + q4 * 4 * 128 * 512,
                                    [[512, 128], [128 * 512, 4], [1, 512]])), 16)
            blk.sync(_sync)

            def _gp(g):
                for q4 in range(4):
                    g.wait_ge(wld.h, wld0 + 16 * (q4 + 1))
                    if q4 >= 2:
                        g.wait_ge(pe.h, pe0 + q4 - 1)
                    gp.inc(g.tensor_copy(out=wbuf[:, (q4 % 2) * 2048:(q4 % 2 + 1) * 2048],
                                         in_=wf32[:, (q4 % 2) * 2048:(q4 % 2 + 1) * 2048]))
            blk.gpsimd(_gp)

            with nc.psum_tensor("psV", [128, 8 * 512], F32) as psV:
                def _t(t):
                    for q4 in range(4):
                        t.wait_ge(gp.h, gp0 + q4 + 1)
                        for tt in range(TT8):
                            for c4 in range(4):
                                c = q4 * 4 + c4
                                i = t.matmul(psV[:, tt * 512:(tt + 1) * 512],
                                             h_bf[:, c * TOK + tt * 128: c * TOK + (tt + 1) * 128],
                                             wbuf[:, (q4 % 2) * 2048 + c4 * 512:(q4 % 2) * 2048 + (c4 + 1) * 512],
                                             start=(c == 0), stop=(c == DC - 1))
                        pe.inc(i)
                blk.tensor(_t)

                def _v(v):
                    v.wait_ge(pe.h, pe0 + 4)
                    for tt in range(TT8):
                        dv.inc(v.tensor_copy(out=vsh[:, tt * 512:(tt + 1) * 512],
                                             in_=psV[:, tt * 512:(tt + 1) * 512]))
                blk.vector(_v)

        sc_open("kvship_mat")
        with nc.Block() as blk:
            base_dv = dv.n

            def _gp(g):
                for p in range(4):
                    gld.inc(g.dma_start(out=k0_in[p * 128:(p + 1) * 128, :],
                                        in_=ksh[:, p * TOK:(p + 1) * TOK]), 16)
                for tt in range(TT8):
                    gld.inc(g.dma_start(out=v0_in[tt * 128:(tt + 1) * 128, :],
                                        in_=vsh[:, tt * 512:(tt + 1) * 512]), 16)
                g.wait_ge(gld.h, gld.n)
                cc.inc(g.collective_compute(
                    "AllGather", ALU.bypass,
                    replica_groups=[[0, 1, 2, 3], [4, 5, 6, 7]],
                    ins=[k0_in.ap().opt()], outs=[k0_all.ap().opt()]))
                cc.inc(g.collective_compute(
                    "AllGather", ALU.bypass,
                    replica_groups=[[0, 1, 2, 3], [4, 5, 6, 7]],
                    ins=[v0_in.ap().opt()], outs=[v0_all.ap().opt()]))
            blk.gpsimd(_gp)
            cc_kv = cc.n

            chunks = ([("g", c) for c in range(DC)] + [("u", c) for c in range(DC)]
                      + [("d", c) for c in range(OT43)])
            base_gld = gld.n
            base_ld = ld.n

            def _sync(sync):
                for i, (kind, c) in enumerate(chunks):
                    w = GSH if kind in "gu" else DSH
                    src = {"g": gidx, "u": uidx, "d": didx}[kind]
                    if i >= 1:
                        sync.wait_ge(dv.h, base_dv + 2 * i - 1)
                    ld.inc(sync.dma_start(out=midx[:, 0:w],
                                          in_=src[c * 128:(c + 1) * 128, :]), 16)
            blk.sync(_sync)

            def _v(v):
                for i, (kind, c) in enumerate(chunks):
                    w = GSH if kind in "gu" else DSH
                    sl = (i % 2) * GSH
                    v.wait_ge(ld.h, base_ld + 16 * (i + 1))
                    if i >= 2:
                        v.wait_ge(gld.h, base_gld + 16 * (i - 1))
                    dv.inc(v.tensor_copy(out=mbf[:, 0:w], in_=midx[:, 0:w]))
                    for j in range(16):
                        v.tensor_scalar(out=mterm[:, 0:w], in0=mbf[:, 0:w],
                                        scalar1=float(j), scalar2=lut128[:, j:j + 1],
                                        op0=ALU.is_equal, op1=ALU.mult)
                        if j == 0:
                            v.tensor_copy(out=macc[:, sl:sl + w], in_=mterm[:, 0:w])
                        elif j == 15:
                            dv.inc(v.tensor_tensor(out=macc[:, sl:sl + w],
                                                   in0=macc[:, sl:sl + w],
                                                   in1=mterm[:, 0:w], op=ALU.add))
                        else:
                            v.tensor_tensor(out=macc[:, sl:sl + w], in0=macc[:, sl:sl + w],
                                            in1=mterm[:, 0:w], op=ALU.add)
            blk.vector(_v)

            def _s(s):
                for i, (kind, c) in enumerate(chunks):
                    w = GSH if kind in "gu" else DSH
                    sl = (i % 2) * GSH
                    dst = {"g": wgt_in, "u": wut_in, "d": wdt_in}[kind]
                    s.wait_ge(dv.h, base_dv + 2 * (i + 1))
                    gld.inc(s.dma_start(out=dst[c * 128:(c + 1) * 128, :],
                                        in_=macc[:, sl:sl + w]), 16)
            blk.scalar(_s)

        sc_open("wag")
        with nc.Block() as blk:
            def _gp(g):
                g.wait_ge(gld.h, gld.n)
                cc.inc(g.collective_compute(
                    "AllGather", ALU.bypass, replica_groups=[list(range(NC))],
                    ins=[wgt_in.ap().opt()], outs=[wgt_all.ap().opt()]))
                cc.inc(g.collective_compute(
                    "AllGather", ALU.bypass, replica_groups=[list(range(NC))],
                    ins=[wut_in.ap().opt()], outs=[wut_all.ap().opt()]))
                cc.inc(g.collective_compute(
                    "AllGather", ALU.bypass, replica_groups=[list(range(NC))],
                    ins=[wdt_in.ap().opt()], outs=[wdt_all.ap().opt()]))
            blk.gpsimd(_gp)
        cc_w = cc.n
        s1b.close()

        sc_open("attn0")
        with ExitStack() as sa:
            EA = sa.enter_context
            k2g = EA(nc.sbuf_tensor("k2g", [128, 2 * FULLT], BF16))
            vg = EA(nc.sbuf_tensor("vg", [128, 2 * 32 * 64], BF16))
            pt0 = EA(nc.sbuf_tensor("pt0", [128, 3 * 1024], BF16))
            psS = EA(nc.psum_tensor("psS", [128, 2048], F32))
            psO = EA(nc.psum_tensor("psO", [128, 512], F32))
            psD = EA(nc.psum_tensor("psD", [128, 512], F32))
            with nc.Block() as blk:
                pe0, ac0, dv0, ld0 = pe.n, ac.n, dv.n, ld.n

                def _sync(sync):
                    sync.wait_ge(cc.h, cc_kv)
                    for g8 in range(HKV):
                        sl = (g8 % 2) * FULLT
                        sv = (g8 % 2) * 2048
                        if g8 >= 2:
                            sync.wait_ge(pe.h, pe0 + 384 * (g8 - 1))
                        for r in range(4):
                            ld.inc(sync.dma_start(
                                out=k2g[0:64, sl + r * TOK: sl + (r + 1) * TOK],
                                in_=k0_all[r * 512 + g8 * 64: r * 512 + (g8 + 1) * 64, :]), 16)
                            ld.inc(sync.dma_start(
                                out=k2g[64:128, sl + r * TOK: sl + (r + 1) * TOK],
                                in_=k0_all[r * 512 + g8 * 64: r * 512 + (g8 + 1) * 64, :]), 16)
                            for tt in range(TT8):
                                ld.inc(sync.dma_start(
                                    out=vg[:, sv + (r * TT8 + tt) * 64: sv + (r * TT8 + tt + 1) * 64],
                                    in_=v0_all[(r * TT8 + tt) * 128:(r * TT8 + tt + 1) * 128,
                                               g8 * 64:(g8 + 1) * 64]), 16)
                blk.sync(_sync)

                def _t(t):
                    svr = [0]
                    def emit_pv(it, kt):
                        eidx = it * 32 + kt
                        t.wait_ge(ac.h, ac0 + eidx + 1)
                        if kt == 0:
                            t.wait_ge(dv.h, dv0 + it)
                        slot = eidx % 3
                        t.matmul(psO[0:64, :], vg[:, svr[0] + kt * 64: svr[0] + (kt + 1) * 64],
                                 pt0[:, slot * 1024: slot * 1024 + 512],
                                 start=(kt == 0), stop=(kt == 31))
                        t.matmul(psO[64:128, :], vg[:, svr[0] + kt * 64: svr[0] + (kt + 1) * 64],
                                 pt0[:, slot * 1024 + 512: slot * 1024 + 1024],
                                 start=(kt == 0), stop=(kt == 31), tile_position=(0, 64))
                        t.matmul(psD[0:64, :], ones_bf[:, 0:64],
                                 pt0[:, slot * 1024: slot * 1024 + 512],
                                 start=(kt == 0), stop=(kt == 31))
                        t.matmul(psD[64:128, :], ones_bf[:, 0:64],
                                 pt0[:, slot * 1024 + 512: slot * 1024 + 1024],
                                 start=(kt == 0), stop=(kt == 31),
                                 tile_position=(0, 64)).then_inc(pe.h, 1)

                    for g8 in range(HKV):
                        sl = (g8 % 2) * FULLT
                        sv = (g8 % 2) * 2048
                        svr[0] = sv
                        t.wait_ge(ld.h, ld0 + 640 * (g8 + 1))
                        for pq in range(2):
                            pair = 2 * g8 + pq
                            for qh in range(2):
                                it = pair * 2 + qh
                                for kt in range(32):
                                    eidx = it * 32 + kt
                                    if eidx >= 1:
                                        t.wait_ge(ac.h, ac0 + eidx - 1)
                                    buf = kt % 2
                                    t.matmul(psS[:, buf * 1024: buf * 1024 + 512],
                                             k2g[0:64, sl + kt * 128: sl + (kt + 1) * 128],
                                             q_fm[0:64, pair * TOK + qh * 512: pair * TOK + (qh + 1) * 512],
                                             start=True, stop=True).then_inc(pe.h, 1)
                                    t.matmul(psS[:, buf * 1024 + 512: buf * 1024 + 1024],
                                             k2g[64:128, sl + kt * 128: sl + (kt + 1) * 128],
                                             q_fm[64:128, pair * TOK + qh * 512: pair * TOK + (qh + 1) * 512],
                                             start=True, stop=True).then_inc(pe.h, 1)
                                    if kt >= 1:
                                        emit_pv(it, kt - 1)
                                emit_pv(it, 31)
                    pe.n += 32 * 96
                blk.tensor(_t)

                def _s(s):
                    for it in range(32):
                        for kt in range(32):
                            eidx = it * 32 + kt
                            buf = kt % 2
                            slot = eidx % 3
                            s.wait_ge(pe.h, pe0 + it * 96 + (3 * kt + 1 if kt >= 1 else 2))
                            s.activation(out=pt0[:, slot * 1024:(slot + 1) * 1024],
                                         in_=psS[:, buf * 1024:(buf + 1) * 1024],
                                         func=AF.Exp, scale=0.125).then_inc(ac.h, 1)
                    ac.n += 32 * 32
                blk.scalar(_s)

                def _v(v):
                    for it in range(32):
                        pair, qh = divmod(it, 2)
                        v.wait_ge(pe.h, pe0 + (it + 1) * 96)
                        v.reciprocal(out=rcp_sb[:, 0:512], in_=psD[:])
                        v.tensor_tensor(
                            out=h_bf[0:64, pair * TOK + qh * 512: pair * TOK + (qh + 1) * 512],
                            in0=psO[0:64, :], in1=rcp_sb[0:64, 0:512], op=ALU.mult)
                        v.tensor_tensor(
                            out=h_bf[64:128, pair * TOK + qh * 512: pair * TOK + (qh + 1) * 512],
                            in0=psO[64:128, :], in1=rcp_sb[64:128, 0:512], op=ALU.mult).then_inc(dv.h, 1)
                    dv.n += 32
                blk.vector(_v)

        sc_open("wo0")
        with (nc.sbuf_tensor("wf32o", [128, 2 * DC * 128], F32) as wf32o,
              nc.sbuf_tensor("wbufo", [128, 2 * DC * 128], BF16) as wbufo,
              nc.psum_tensor("psPo", [128, 2048], F32) as psP):
            def evwo(v, ot, qc, pss):
                return v.tensor_tensor(
                    out=x_fm[:, ot * TOK + qc * 512: ot * TOK + (qc + 1) * 512],
                    in0=x_fm[:, ot * TOK + qc * 512: ot * TOK + (qc + 1) * 512],
                    in1=pss, op=ALU.add)
            emit_proj(wo0, D, wf32o, wbufo, psP, evwo)

    if stages >= 2:
        with ExitStack() as s2:
            E2 = s2.enter_context
            inter_sb = E2(nc.sbuf_tensor("inter_sb", [128, OT43 * 512], BF16))
            wgu = E2(nc.sbuf_tensor("wgu", [128, 2 * 2 * DC * 128], BF16))
            wdd = E2(nc.sbuf_tensor("wdd", [128, 2 * OT43 * 128], BF16))
            gstage = E2(nc.sbuf_tensor("gstage", [128, 2 * 512], BF16))

            sc_open("ln2")
            with (nc.psum_tensor("psA2", [128, 1024], F32) as psA,
                  nc.psum_tensor("psB2", [128, 1024], F32) as psB):
                emit_ln(psA, psB, inter_sb[:, 0:DC * TOK], h_bf[:, 0:DC * TOK])

            sc_open("ffn0")
            for qh in range(2):
                with (nc.psum_tensor(f"psG{qh}", [128, 1024], F32) as psG,
                      nc.psum_tensor(f"psU{qh}", [128, 1024], F32) as psU):
                    with nc.Block() as blk:
                        wld0, pe0, ac0, dv0 = wld.n, pe.n, ac.n, dv.n

                        def _sync(sync):
                            if qh == 0:
                                sync.wait_ge(cc.h, cc_w)
                            for ot in range(OT43):
                                r, lo = divmod(ot, 6)
                                if ot >= 2:
                                    sync.wait_ge(pe.h, pe0 + 2 * (ot - 1))
                                wld.inc(sync.dma_start(
                                    out=wgu[:, (ot % 2) * 4096: (ot % 2) * 4096 + 2048],
                                    in_=bass.AP(wgt_all.ap().tensor,
                                                wgt_all.ap().offset + (r * D) * GSH + lo * 128,
                                                [[GSH, 128], [128 * GSH, DC], [1, 128]])), 16)
                                wld.inc(sync.dma_start(
                                    out=wgu[:, (ot % 2) * 4096 + 2048: (ot % 2 + 1) * 4096],
                                    in_=bass.AP(wut_all.ap().tensor,
                                                wut_all.ap().offset + (r * D) * GSH + lo * 128,
                                                [[GSH, 128], [128 * GSH, DC], [1, 128]])), 16)
                        blk.sync(_sync)

                        def _t(t):
                            for ot in range(OT43):
                                t.wait_ge(wld.h, wld0 + 32 * (ot + 1))
                                pb = ot % 2
                                if ot >= 2:
                                    t.wait_ge(dv.h, dv0 + (ot - 2) + 1)
                                    t.wait_ge(ac.h, ac0 + (ot - 2) + 1)
                                for c in range(DC):
                                    i = t.matmul(psG[:, pb * 512:(pb + 1) * 512],
                                                 wgu[:, (ot % 2) * 4096 + c * 128:(ot % 2) * 4096 + (c + 1) * 128],
                                                 h_bf[:, c * TOK + qh * 512: c * TOK + (qh + 1) * 512],
                                                 start=(c == 0), stop=(c == DC - 1))
                                pe.inc(i)
                                for c in range(DC):
                                    i = t.matmul(psU[:, pb * 512:(pb + 1) * 512],
                                                 wgu[:, (ot % 2) * 4096 + 2048 + c * 128:(ot % 2) * 4096 + 2048 + (c + 1) * 128],
                                                 h_bf[:, c * TOK + qh * 512: c * TOK + (qh + 1) * 512],
                                                 start=(c == 0), stop=(c == DC - 1))
                                pe.inc(i)
                        blk.tensor(_t)

                        def _s(s):
                            for ot in range(OT43):
                                pb = ot % 2
                                s.wait_ge(pe.h, pe0 + 2 * ot + 1)
                                if ot >= 2:
                                    s.wait_ge(dv.h, dv0 + (ot - 2) + 1)
                                s.activation(out=gstage[:, pb * 512:(pb + 1) * 512],
                                             in_=psG[:, pb * 512:(pb + 1) * 512],
                                             func=AF.Silu).then_inc(ac.h, 1)
                            ac.n += OT43
                        blk.scalar(_s)

                        def _v(v):
                            for ot in range(OT43):
                                pb = ot % 2
                                v.wait_ge(pe.h, pe0 + 2 * ot + 2)
                                v.wait_ge(ac.h, ac0 + ot + 1)
                                v.tensor_tensor(out=inter_sb[:, ot * 512:(ot + 1) * 512],
                                                in0=gstage[:, pb * 512:(pb + 1) * 512],
                                                in1=psU[:, pb * 512:(pb + 1) * 512],
                                                op=ALU.mult).then_inc(dv.h, 1)
                            dv.n += OT43
                        blk.vector(_v)

                with nc.psum_tensor(f"psDn{qh}", [128, 1024], F32) as psDn:
                    with nc.Block() as blk:
                        wld0, pe0, dv0 = wld.n, pe.n, dv.n

                        def _sync(sync):
                            for o2 in range(DC):
                                r2, lo2 = divmod(o2 * 128, DSH)
                                if o2 >= 2:
                                    sync.wait_ge(pe.h, pe0 + (o2 - 1))
                                wld.inc(sync.dma_start(
                                    out=wdd[:, (o2 % 2) * OT43 * 128:(o2 % 2 + 1) * OT43 * 128],
                                    in_=bass.AP(wdt_all.ap().tensor,
                                                wdt_all.ap().offset + r2 * INTER * DSH + lo2,
                                                [[DSH, 128], [128 * DSH, OT43], [1, 128]])), 16)
                        blk.sync(_sync)

                        def _t(t):
                            for o2 in range(DC):
                                t.wait_ge(wld.h, wld0 + 16 * (o2 + 1))
                                pb = o2 % 2
                                if o2 >= 2:
                                    t.wait_ge(dv.h, dv0 + (o2 - 2) + 1)
                                for hc in range(OT43):
                                    i = t.matmul(psDn[:, pb * 512:(pb + 1) * 512],
                                                 wdd[:, (o2 % 2) * OT43 * 128 + hc * 128:(o2 % 2) * OT43 * 128 + (hc + 1) * 128],
                                                 inter_sb[:, hc * 512:(hc + 1) * 512],
                                                 start=(hc == 0), stop=(hc == OT43 - 1))
                                pe.inc(i)
                        blk.tensor(_t)

                        def _v(v):
                            for o2 in range(DC):
                                pb = o2 % 2
                                v.wait_ge(pe.h, pe0 + o2 + 1)
                                v.tensor_tensor(
                                    out=x_fm[:, o2 * TOK + qh * 512: o2 * TOK + (qh + 1) * 512],
                                    in0=x_fm[:, o2 * TOK + qh * 512: o2 * TOK + (qh + 1) * 512],
                                    in1=psDn[:, pb * 512:(pb + 1) * 512],
                                    op=ALU.add).then_inc(dv.h, 1)
                            dv.n += DC
                        blk.vector(_v)

    if stages >= 3:
        with ExitStack() as s3:
            E3 = s3.enter_context
            qf_bf = E3(nc.sbuf_tensor("qf_bf", [65, TOK], BF16))
            m8_bf = E3(nc.sbuf_tensor("m8_bf", [128, 8], BF16))
            mcols = E3(nc.sbuf_tensor("mcols", [128, 64], F32))
            mrow_sb = E3(nc.sbuf_tensor("mrow_sb", [1, TOK], BF16))
            kfsh = E3(nc.sbuf_tensor("kfsh", [64, TOK], BF16))
            s3b = s3.enter_context(ExitStack())
            E3b = s3b.enter_context
            v1sh = E3b(nc.sbuf_tensor("v1sh", [128, TT8 * D], BF16))
            wf3 = E3b(nc.sbuf_tensor("wf3", [128, 2 * DC * 128], F32))
            wb3 = E3b(nc.sbuf_tensor("wb3", [128, 2 * DC * 128], BF16))

            sc_open("ln3")
            with (nc.psum_tensor("psA3", [128, 1024], F32) as psA,
                  nc.psum_tensor("psB3", [128, 1024], F32) as psB):
                emit_ln(psA, psB, v1sh[:, 0:DC * TOK], h_bf[:, 0:DC * TOK])

            sc_open("proj1")
            with nc.psum_tensor("psQf", [64, 2 * 1024], F32) as psQf:
                with nc.Block() as blk:
                    wld0, gp0, pe0 = wld.n, gp.n, pe.n

                    def _sync(sync):
                        for m, w_ap in enumerate((wq1, wk1)):
                            wld.inc(sync.dma_start(
                                out=wf3[:, m * (DC * 64):(m + 1) * (DC * 64)],
                                in_=bass.AP(w_ap.tensor, w_ap.offset,
                                            [[64, 128], [128 * 64, DC], [1, 64]])), 16)
                    blk.sync(_sync)

                    def _gp(g):
                        for m in range(2):
                            g.wait_ge(wld.h, wld0 + 16 * (m + 1))
                            gp.inc(g.tensor_copy(out=wb3[:, m * 1024:(m + 1) * 1024],
                                                 in_=wf3[:, m * 1024:(m + 1) * 1024]))
                    blk.gpsimd(_gp)

                    def _t(t):
                        for m in range(2):
                            t.wait_ge(gp.h, gp0 + m + 1)
                            for qc in range(2):
                                for c in range(DC):
                                    i = t.matmul(
                                        psQf[:, m * 1024 + qc * 512: m * 1024 + (qc + 1) * 512],
                                        wb3[:, m * 1024 + c * 64: m * 1024 + (c + 1) * 64],
                                        h_bf[:, c * TOK + qc * 512: c * TOK + (qc + 1) * 512],
                                        start=(c == 0), stop=(c == DC - 1))
                                pe.inc(i)
                    blk.tensor(_t)

                    def _v(v):
                        v.wait_ge(pe.h, pe0 + 4)
                        v.tensor_copy(out=qf_bf[0:64, :], in_=psQf[:, 0:1024])
                        dv.inc(v.tensor_copy(out=kfsh[:], in_=psQf[:, 1024:2048]))
                    blk.vector(_v)

            with nc.psum_tensor("psV1", [128, TT8 * 512], F32) as psV1:
                for oc in range(4):
                    with nc.Block() as blk:
                        wld0, gp0, pe0 = wld.n, gp.n, pe.n

                        def _sync(sync, oc=oc):
                            for q4 in range(4):
                                if q4 >= 2:
                                    sync.wait_ge(pe.h, pe0 + q4 - 1)
                                wld.inc(sync.dma_start(
                                    out=wf3[:, (q4 % 2) * 2048:(q4 % 2 + 1) * 2048],
                                    in_=bass.AP(wv1.tensor,
                                                wv1.offset + q4 * 4 * 128 * D + oc * 512,
                                                [[D, 128], [128 * D, 4], [1, 512]])), 16)
                        blk.sync(_sync)

                        def _gp(g):
                            for q4 in range(4):
                                g.wait_ge(wld.h, wld0 + 16 * (q4 + 1))
                                if q4 >= 2:
                                    g.wait_ge(pe.h, pe0 + q4 - 1)
                                gp.inc(g.tensor_copy(
                                    out=wb3[:, (q4 % 2) * 2048:(q4 % 2 + 1) * 2048],
                                    in_=wf3[:, (q4 % 2) * 2048:(q4 % 2 + 1) * 2048]))
                        blk.gpsimd(_gp)

                        def _t(t):
                            for q4 in range(4):
                                t.wait_ge(gp.h, gp0 + q4 + 1)
                                for tt in range(TT8):
                                    for c4 in range(4):
                                        c = q4 * 4 + c4
                                        i = t.matmul(
                                            psV1[:, tt * 512:(tt + 1) * 512],
                                            h_bf[:, c * TOK + tt * 128: c * TOK + (tt + 1) * 128],
                                            wb3[:, (q4 % 2) * 2048 + c4 * 512:(q4 % 2) * 2048 + (c4 + 1) * 512],
                                            start=(c == 0), stop=(c == DC - 1))
                                pe.inc(i)
                        blk.tensor(_t)

                        def _v(v, oc=oc):
                            v.wait_ge(pe.h, pe0 + 4)
                            for tt in range(TT8):
                                dv.inc(v.tensor_copy(
                                    out=v1sh[:, tt * D + oc * 512: tt * D + (oc + 1) * 512],
                                    in_=psV1[:, tt * 512:(tt + 1) * 512]))
                        blk.vector(_v)

            sc_open("kvship1")
            with nc.Block() as blk:
                def _gp(g):
                    gld.inc(g.dma_start(out=kf_in[:, :], in_=kfsh[:]), 16)
                    for tt in range(TT8):
                        gld.inc(g.dma_start(out=v1_in[tt * 128:(tt + 1) * 128, :],
                                            in_=v1sh[:, tt * D:(tt + 1) * D]), 16)
                    g.wait_ge(gld.h, gld.n)
                    cc.inc(g.collective_compute(
                        "AllGather", ALU.bypass,
                        replica_groups=[[0, 1, 2, 3], [4, 5, 6, 7]],
                        ins=[kf_in.ap().opt()], outs=[kf_all.ap().opt()]))
                    cc.inc(g.collective_compute(
                        "AllGather", ALU.bypass,
                        replica_groups=[[0, 1, 2, 3], [4, 5, 6, 7]],
                        ins=[v1_in.ap().opt()], outs=[v1_all.ap().opt()]))
                blk.gpsimd(_gp)
            cc_kv1 = cc.n
            s3b.close()

            with ExitStack() as sa3:
                EA3 = sa3.enter_context
                kf_full = EA3(nc.sbuf_tensor("kf_full", [65, FULLT], BF16))
                pt1 = EA3(nc.sbuf_tensor("pt1", [128, 32 * 512], BF16))
                vt_buf = EA3(nc.sbuf_tensor("vt_buf", [128, 2 * 4096], BF16))
                psS1 = EA3(nc.psum_tensor("psS1", [128, 2 * 512], F32))
                psDen1 = EA3(nc.psum_tensor("psDen1", [128, 512], F32))
                psO3 = EA3(nc.psum_tensor("psO3", [128, 2 * 512], F32))

                sc_open("mstab")
                with nc.psum_tensor("psMt", [128, 1536], F32) as psMt:
                    with nc.Block() as blk:
                        ld0m, pe0m, dv0m = ld.n, pe.n, dv.n

                        def _sync(sync):
                            sync.wait_ge(cc.h, cc_kv1)
                            for r in range(4):
                                ld.inc(sync.dma_start(
                                    out=kf_full[0:64, r * TOK:(r + 1) * TOK],
                                    in_=kf_all[r * 64:(r + 1) * 64, :]), 16)
                        blk.sync(_sync)
                        ldm = ld.n

                        def _v0(v):
                            v.wait_ge(ld.h, ldm)
                            dv.inc(v.memset(kf_full[64:65, :], 1.0))
                            for i in range(64):
                                v.wait_ge(pe.h, pe0m + i + 1)
                                dv.inc(v.reduce_max(out=mcols[:, i:i + 1],
                                                    in_=psMt[:, (i % 3) * 512:(i % 3 + 1) * 512],
                                                    axis=mybir.AxisListType.X))
                            for qt in range(TT8):
                                i = v.reduce_max(out=tmp_sb[:, qt:qt + 1],
                                                 in_=mcols[:, qt * 8:(qt + 1) * 8],
                                                 axis=mybir.AxisListType.X)
                            dv.inc(v.tensor_copy(out=m8_bf[:], in_=tmp_sb[:, 0:8]))
                        blk.vector(_v0)

                        def _t0(t):
                            t.wait_ge(dv.h, dv0m + 1)
                            for i in range(64):
                                qt, kc = divmod(i, 8)
                                if i >= 3:
                                    t.wait_ge(dv.h, dv0m + i - 1)
                                t.matmul(psMt[:, (i % 3) * 512:(i % 3 + 1) * 512],
                                         qf_bf[0:64, qt * 128:(qt + 1) * 128],
                                         kf_full[0:64, kc * 512:(kc + 1) * 512],
                                         start=True, stop=True).then_inc(pe.h, 1)
                            pe.n += 64
                        blk.tensor(_t0)
                with (nc.psum_tensor("psMrow", [1, 1024], F32) as psMrow,
                      nc.Block() as blk):
                    pe0m, dv0m = pe.n, dv.n

                    def _t0(t):
                        for qt in range(TT8):
                            i = t.matmul(psMrow[0:1, qt * 128:(qt + 1) * 128],
                                         m8_bf[:, qt:qt + 1], ident_bf[:, :],
                                         start=True, stop=True)
                        pe.inc(i)
                    blk.tensor(_t0)

                    def _v0(v):
                        v.wait_ge(pe.h, pe0m + 1)
                        dv.inc(v.tensor_scalar(out=mrow_sb[:], in0=psMrow[0:1, :],
                                               scalar1=-1.0, scalar2=None, op0=ALU.mult))
                    blk.vector(_v0)

                    def _g0(g):
                        g.wait_ge(dv.h, dv0m + 1)
                        gld.inc(g.dma_start(out=qf_bf[64:65, :], in_=mrow_sb[0:1, :]), 16)
                        g.wait_ge(gld.h, gld.n)
                    blk.gpsimd(_g0)

                sc_open("attn1")
                for qh in range(2):
                    with nc.Block() as blk:
                        ld0, pe0, ac0, dv0 = ld.n, pe.n, ac.n, dv.n

                        def _sync(sync, qh=qh):
                            pass
                        blk.sync(_sync)
                        ldt = ld.n

                        def _t(t, qh=qh):
                            t.wait_ge(ld.h, ldt)
                            def emit_den(kt):
                                t.wait_ge(ac.h, ac0 + kt + 1)
                                t.matmul(psDen1[:], ones_bf[:],
                                         pt1[:, kt * 512:(kt + 1) * 512],
                                         start=(kt == 0), stop=(kt == 31)).then_inc(pe.h, 1)
                            for kt in range(32):
                                if kt >= 2:
                                    t.wait_ge(ac.h, ac0 + kt - 1)
                                t.matmul(psS1[:, (kt % 2) * 512:(kt % 2 + 1) * 512],
                                         kf_full[0:65, kt * 128:(kt + 1) * 128],
                                         qf_bf[0:65, qh * 512:(qh + 1) * 512],
                                         start=True, stop=True).then_inc(pe.h, 1)
                                if kt >= 1:
                                    emit_den(kt - 1)
                            emit_den(31)
                            pe.n += 64
                        blk.tensor(_t)

                        def _s(s):
                            for kt in range(32):
                                s.wait_ge(pe.h, pe0 + (2 * kt if kt >= 1 else 1))
                                s.activation(out=pt1[:, kt * 512:(kt + 1) * 512],
                                             in_=psS1[:, (kt % 2) * 512:(kt % 2 + 1) * 512],
                                             func=AF.Exp).then_inc(ac.h, 1)
                            ac.n += 32
                        blk.scalar(_s)

                        def _v(v, qh=qh):
                            v.wait_ge(pe.h, pe0 + 64)
                            dv.inc(v.reciprocal(out=rcp_sb[:, qh * 512:(qh + 1) * 512],
                                                in_=psDen1[:]))
                        blk.vector(_v)

                    with nc.Block() as blk:
                        wld0, pe0, dv0 = wld.n, pe.n, dv.n

                        def _sync(sync):
                            for oc in range(DC):
                                if oc >= 2:
                                    sync.wait_ge(pe.h, pe0 + oc - 1)
                                for r in range(4):
                                    wld.inc(sync.dma_start(
                                        out=vt_buf[:, (oc % 2) * 4096 + r * 1024:(oc % 2) * 4096 + (r + 1) * 1024],
                                        in_=bass.AP(v1_all.ap().tensor,
                                                    v1_all.ap().offset + r * TOK * D + oc * 128,
                                                    [[D, 128], [128 * D, TT8], [1, 128]])), 16)
                        blk.sync(_sync)

                        def _t(t):
                            for oc in range(DC):
                                t.wait_ge(wld.h, wld0 + 64 * (oc + 1))
                                if oc >= 2:
                                    t.wait_ge(dv.h, dv0 + oc - 1)
                                for kt in range(32):
                                    i = t.matmul(
                                        psO3[:, (oc % 2) * 512:(oc % 2 + 1) * 512],
                                        vt_buf[:, (oc % 2) * 4096 + kt * 128:(oc % 2) * 4096 + (kt + 1) * 128],
                                        pt1[:, kt * 512:(kt + 1) * 512],
                                        start=(kt == 0), stop=(kt == 31))
                                pe.inc(i)
                        blk.tensor(_t)

                        def _v(v, qh=qh):
                            for oc in range(DC):
                                v.wait_ge(pe.h, pe0 + oc + 1)
                                v.tensor_tensor(
                                    out=h_bf[:, oc * TOK + qh * 512: oc * TOK + (qh + 1) * 512],
                                    in0=psO3[:, (oc % 2) * 512:(oc % 2 + 1) * 512],
                                    in1=rcp_sb[:, qh * 512:(qh + 1) * 512],
                                    op=ALU.mult).then_inc(dv.h, 1)
                            dv.n += DC
                        blk.vector(_v)

            sc_open("wo1")
            with (nc.sbuf_tensor("wf3o", [128, 2 * DC * 128], F32) as wf3o,
                  nc.sbuf_tensor("wb3o", [128, 2 * DC * 128], BF16) as wb3o,
                  nc.psum_tensor("psP3", [128, 2048], F32) as psP3):
                def evwo1(v, ot, qc, pss):
                    return v.tensor_tensor(
                        out=x_fm[:, ot * TOK + qc * 512: ot * TOK + (qc + 1) * 512],
                        in0=x_fm[:, ot * TOK + qc * 512: ot * TOK + (qc + 1) * 512],
                        in1=pss, op=ALU.add)
                emit_proj(wo1, D, wf3o, wb3o, psP3, evwo1)

    if stages >= 4:
        with ExitStack() as s4:
            E4 = s4.enter_context
            r_sb = E4(nc.sbuf_tensor("r_sb", [128, DC * TOK], BF16))
            wf4 = E4(nc.sbuf_tensor("wf4", [128, 2 * DC * 128], F32))
            wb4 = E4(nc.sbuf_tensor("wb4", [128, 2 * DC * 128], BF16))
            lg_bf = E4(nc.sbuf_tensor("lg_bf", [8, TOK], BF16))
            lt_f = E4(nc.sbuf_tensor("lt_f", [128, 64], F32))
            m1_sb = E4(nc.sbuf_tensor("m1_sb", [128, 8], F32))
            md_sb = E4(nc.sbuf_tensor("md_sb", [128, 8], F32))
            ga_sb = E4(nc.sbuf_tensor("ga_sb", [128, 64], F32))
            gb_sb = E4(nc.sbuf_tensor("gb_sb", [128, 64], F32))
            e_sb = E4(nc.sbuf_tensor("e_sb", [128, 8], F32))
            S_bf = E4(nc.sbuf_tensor("S_bf", [128, 8], BF16))
            srow_bf = E4(nc.sbuf_tensor("srow_bf", [1, TOK], BF16))
            srep_bf = E4(nc.sbuf_tensor("srep_bf", [128, TOK], BF16))

            sc_open("ln4")
            with (nc.psum_tensor("psA4", [128, 1024], F32) as psA,
                  nc.psum_tensor("psB4", [128, 1024], F32) as psB):
                emit_ln(psA, psB, r_sb[:, 0:DC * TOK], h_bf[:, 0:DC * TOK])

            sc_open("gate4")
            with (nc.psum_tensor("psL", [8, 1024], F32) as psL,
                  nc.psum_tensor("psT4", [128, 64], BF16) as psT4,
                  nc.psum_tensor("psRow", [1, 1024], F32) as psRow,
                  nc.psum_tensor("psSrep", [128, 1024], F32) as psSrep):
                with nc.Block() as blk:
                    wld0, gp0, pe0, dv0, ac0 = wld.n, gp.n, pe.n, dv.n, ac.n

                    def _sync(sync):
                        wld.inc(sync.dma_start(
                            out=wf4[:, 0:DC * 8],
                            in_=bass.AP(wg8.tensor, wg8.offset,
                                        [[8, 128], [128 * 8, DC], [1, 8]])), 16)
                    blk.sync(_sync)

                    def _gp(g):
                        g.wait_ge(wld.h, wld0 + 16)
                        gp.inc(g.tensor_copy(out=wb4[:, 0:DC * 8], in_=wf4[:, 0:DC * 8]))
                    blk.gpsimd(_gp)

                    def _t(t):
                        t.wait_ge(gp.h, gp0 + 1)
                        for qc in range(2):
                            for c in range(DC):
                                i = t.matmul(psL[:, qc * 512:(qc + 1) * 512],
                                             wb4[:, c * 8:(c + 1) * 8],
                                             h_bf[:, c * TOK + qc * 512: c * TOK + (qc + 1) * 512],
                                             start=(c == 0), stop=(c == DC - 1))
                            pe.inc(i)
                        t.wait_ge(dv.h, dv0 + 1)
                        for j in range(8):
                            i = t.transpose(psT4[:, j * 8:(j + 1) * 8],
                                            lg_bf[0:8, j * 128:(j + 1) * 128],
                                            ident_bf[0:8, 0:8])
                        pe.inc(i)
                        t.wait_ge(dv.h, dv0 + 3)
                        for j in range(8):
                            i = t.matmul(psRow[0:1, j * 128:(j + 1) * 128],
                                         S_bf[:, j:j + 1], ident_bf[:, :],
                                         start=True, stop=True)
                        pe.inc(i)
                        t.wait_ge(dv.h, dv0 + 4)
                        for qc in range(2):
                            i = t.matmul(psSrep[:, qc * 512:(qc + 1) * 512],
                                         ones_bf[0:1, :], srow_bf[0:1, qc * 512:(qc + 1) * 512],
                                         start=True, stop=True)
                        pe.inc(i)
                    blk.tensor(_t)

                    def _v(v):
                        v.wait_ge(pe.h, pe0 + 2)
                        dv.inc(v.tensor_copy(out=lg_bf[:], in_=psL[:, 0:1024]))
                        v.wait_ge(pe.h, pe0 + 3)
                        v.tensor_copy(out=lt_f[:], in_=psT4[:])
                        for j in range(8):
                            v.reduce_max(out=m1_sb[:, j:j + 1], in_=lt_f[:, j * 8:(j + 1) * 8],
                                         axis=mybir.AxisListType.X)
                        v.tensor_copy(out=ga_sb[:], in_=lt_f[:])
                        v.tensor_copy(out=gb_sb[:], in_=lt_f[:])
                        for j in range(8):
                            v.tensor_scalar_sub(out=ga_sb[:, j * 8:(j + 1) * 8],
                                                in0=lt_f[:, j * 8:(j + 1) * 8],
                                                scalar1=m1_sb[:, j:j + 1])
                        v.tensor_scalar(out=gb_sb[:], in0=ga_sb[:], scalar1=0.0,
                                        scalar2=-1.0e30, op0=ALU.is_equal, op1=ALU.mult)
                        v.tensor_tensor(out=gb_sb[:], in0=gb_sb[:], in1=ga_sb[:], op=ALU.add)
                        for j in range(8):
                            i = v.reduce_max(out=md_sb[:, j:j + 1], in_=gb_sb[:, j * 8:(j + 1) * 8],
                                             axis=mybir.AxisListType.X)
                        dv.inc(i)
                        v.wait_ge(ac.h, ac0 + 2)
                        dv.inc(v.tensor_tensor(out=S_bf[:], in0=e_sb[:], in1=ga_sb[:, 0:8],
                                               op=ALU.add))
                        v.wait_ge(pe.h, pe0 + 4)
                        dv.inc(v.tensor_copy(out=srow_bf[:], in_=psRow[0:1, :]))
                        v.wait_ge(pe.h, pe0 + 5)
                        dv.inc(v.tensor_copy(out=srep_bf[:], in_=psSrep[:, 0:1024]))
                    blk.vector(_v)

                    def _s(s):
                        s.wait_ge(dv.h, dv0 + 2)
                        ac.inc(s.activation(out=e_sb[:], in_=md_sb[:], func=AF.Sigmoid))
                        ac.inc(s.activation(out=ga_sb[:, 0:8], in_=md_sb[:], func=AF.Sigmoid,
                                            scale=-1.0))
                    blk.scalar(_s)

            sc_open("updown")
            with nc.psum_tensor("psP4", [128, 2048], F32) as psP4:
                def evrelu(v, ot, qc, pss):
                    return v.tensor_scalar_max(
                        out=r_sb[:, ot * TOK + qc * 512: ot * TOK + (qc + 1) * 512],
                        in0=pss, scalar1=0.0)
                emit_proj(wup, D, wf4, wb4, psP4, evrelu)

                def evdn(v, ot, qc, pss):
                    return v.tensor_tensor(
                        out=x_fm[:, ot * TOK + qc * 512: ot * TOK + (qc + 1) * 512],
                        in0=x_fm[:, ot * TOK + qc * 512: ot * TOK + (qc + 1) * 512],
                        in1=pss, op=ALU.add)
                emit_proj(wdn, D, wf4, wb4, psP4, evdn, rhs=r_sb)

            sc_open("expert")
            with nc.Block() as blk:
                def _v(v):
                    for c in range(DC):
                        i = v.tensor_tensor(out=h_bf[:, c * TOK:(c + 1) * TOK],
                                            in0=h_bf[:, c * TOK:(c + 1) * TOK],
                                            in1=srep_bf[:], op=ALU.mult)
                    dv.inc(i)
                blk.vector(_v)
            with nc.psum_tensor("psP5", [128, 2048], F32) as psP5:
                def evex(v, ot, qc, pss):
                    return v.tensor_tensor(
                        out=x_fm[:, ot * TOK + qc * 512: ot * TOK + (qc + 1) * 512],
                        in0=x_fm[:, ot * TOK + qc * 512: ot * TOK + (qc + 1) * 512],
                        in1=pss, op=ALU.add)
                emit_proj(wex, D, wf4, wb4, psP5, evex)

    sc_open("p_store")
    with nc.Block() as blk:
        def _sync(sync):
            for c in range(DC):
                st.inc(sync.dma_start(out=ys[c * 128:(c + 1) * 128, :],
                                      in_=x_fm[:, c * TOK:(c + 1) * TOK]), 16)
            sync.wait_ge(st.h, st.n)
        blk.sync(_sync)

    sc_close()
    ctx.close()
    return nc


def kernel(**inputs):
    ii = {k: np.asarray(v) for k, v in inputs.items()}
    x = ii["L0_attn_x"].astype(np.float32, copy=False)
    C = np.ascontiguousarray

    gate_idx = ii["L0_ffn_gate_idx"].astype(np.int32, copy=False)
    up_idx = ii["L0_ffn_up_idx"].astype(np.int32, copy=False)
    down_idx = ii["L0_ffn_down_idx"].astype(np.int32, copy=False)
    gpad = np.zeros((NC * GSH, D), np.int32)
    gpad[:INTER] = gate_idx
    upad = np.zeros((NC * GSH, D), np.int32)
    upad[:INTER] = up_idx

    shared = {
        "wq0": C(ii["L0_attn_Wq"].astype(np.float32, copy=False)),
        "wk0": C(ii["L0_attn_Wk"].astype(np.float32, copy=False)),
        "wv0": C(ii["L0_attn_Wv"].astype(np.float32, copy=False)),
        "wo0": C(ii["L0_attn_Wo"].astype(np.float32, copy=False)),
        "lut": C(ii["L0_ffn_lut"].astype(np.float32, copy=False)),
        "wq1": C(ii["L1_attn_Wq"].astype(np.float32, copy=False).T),
        "wk1": C(ii["L1_attn_Wk"].astype(np.float32, copy=False).T),
        "wv1": C(ii["L1_attn_Wv"].astype(np.float32, copy=False).T),
        "wo1": C(ii["L1_attn_Wo"].astype(np.float32, copy=False).T),
        "wup": C(ii["L1_ffn_W_up"].astype(np.float32, copy=False).T),
        "wdn": C(ii["L1_ffn_W_down"].astype(np.float32, copy=False).T),
        "wg8": C(ii["L1_ffn_W_gate"].astype(np.float32, copy=False).T),
        "wex": C(ii["L1_ffn_W_expert"].astype(np.float32, copy=False)),
    }
    in_maps = []
    for c in range(NC):
        b, qi = divmod(c, 4)
        m = dict(shared)
        m["xs"] = C(x[b, qi * TOK:(qi + 1) * TOK, :].T)
        m["gidx"] = C(gpad[c * GSH:(c + 1) * GSH, :].T)
        m["uidx"] = C(upad[c * GSH:(c + 1) * GSH, :].T)
        m["didx"] = C(down_idx[c * DSH:(c + 1) * DSH, :].T)
        in_maps.append(m)

    nc_prog = build_nc()
    res = run_bass_kernel_spmd(nc_prog, in_maps, core_ids=list(range(NC)))
    y = np.empty((2, FULLT, D), np.float32)
    for c in range(NC):
        b, qi = divmod(c, 4)
        y[b, qi * TOK:(qi + 1) * TOK, :] = res.results[c]["ys"].T
    return y
